# version 16
# baseline (speedup 1.0000x reference)
import contextlib
import math
import numpy as np
import ml_dtypes
import concourse.bass as bass
import concourse.mybir as mybir
from concourse.bass_utils import run_bass_kernel_spmd

F32 = mybir.dt.float32
BF16 = mybir.dt.bfloat16
AF = mybir.ActivationFunctionType
ALU = mybir.AluOpType
AX = mybir.AxisListType

MASKVAL = -66504.0
EPS = 1e-5
ROPE_BASE = 10000.0


class Cfg:
    def __init__(self, D, S, H, F):
        self.D, self.S, self.H, self.F = D, S, H, F
        self.HD = 128
        assert D == H * self.HD
        self.M = F // 2
        self.DC = D // 128
        self.MC = self.M // 128
        self.FC = F // 128
        self.TT = S // 8
        assert self.TT <= 512
        self.CPT = self.TT // 128
        self.OWN = S // 2
        self.GRP = self.OWN
        self.NB = self.GRP // 128
        self.FT = min(512, self.GRP)
        self.TAU = self.GRP // self.FT
        self.SEC = self.OWN // 128
        self.NCH = self.S // 128
        self.DT = D // 512
        self.GP4 = min(1024, self.OWN)
        self.NG4 = self.OWN // self.GP4
        self.FT4 = min(512, self.GP4)
        self.T4 = self.GP4 // self.FT4
        self.QSB = min(512, self.GP4)
        self.NQS = self.GP4 // self.QSB
        self.QB = self.QSB // 128
        self.scale = 1.0 / math.sqrt(128.0)
        self.kvx = True
        self.ncc = 8


CFG_FULL = Cfg(4096, 4096, 32, 8192)
CFG_SIM = Cfg(512, 1024, 4, 1024)
CFG_MID = Cfg(1024, 2048, 8, 2048)

ENGS = ["sync", "gpsimd", "tensor", "scalar", "vector"]


class Em:
    def __init__(self, nc, semnames, stack):
        self.nc = nc
        self.semnames = list(semnames)
        self.sems = {n: stack.enter_context(nc.semaphore(n)) for n in semnames}

    def begin(self, eng, E):
        self.eng = eng
        self.E = E
        self.c = {n: 0 for n in self.sems}
        self.waited = {n: 0 for n in self.sems}
        self.ai = 0
        self.first_pass = eng == ENGS[0]
        if self.first_pass:
            self.allocs = []

    def on(self, *names):
        return self.eng in names

    def w(self, name, val):
        if val is None or val <= 0:
            return
        if val > self.waited[name]:
            self.E.wait_ge(self.sems[name], val)
            self.waited[name] = val

    def dep(self, deps):
        for d in deps:
            if d is not None and d[1] is not None:
                self.w(d[0], d[1])

    def op(self, eng, deps, f, mark=None, n=1):
        ins = None
        if self.on(eng):
            self.dep(deps)
            ins = f()
        if mark is not None:
            self.c[mark] += n
            if ins is not None:
                ins.then_inc(self.sems[mark], n)
            return self.c[mark]
        return None

    def barrier(self, skip=()):
        for nm in self.semnames:
            if nm not in skip:
                self.w(nm, self.c[nm])


SEMNAMES = (
    ["pe", "act", "dve", "pool", "dm", "dmg", "gw", "gw2", "gx", "cc"]
    + ["dx0", "dx1", "ws0", "ws1", "wf0", "wf1"]
    + ["sq0", "sq1", "sk0", "sk1", "sv0", "sv1", "sxn", "sc0", "sc1"]
    + ["dk0", "dk1", "dv0", "dv1", "du0", "du1"]
    + ["dxn", "dw0", "dw1", "dn0", "dn1", "do0", "do1", "dc0"]
    + ["so0", "so1"]
)




_EM = None


class _CachedGuard:

    def __init__(self, factory):
        self.factory = factory
        self.g = None

    def __enter__(self):
        em = _EM
        i = em.ai
        em.ai += 1
        if em.first_pass:
            self.g = self.factory()
            h = self.g.__enter__()
            em.allocs.append(h)
            return h
        return em.allocs[i]

    def __exit__(self, *a):
        if self.g is not None:
            return self.g.__exit__(*a)
        return False


def _sb(nc, name, shape, dt):
    return _CachedGuard(lambda: nc.sbuf_tensor(name, shape, dt, allow_name_mangling=True))


def _ps(nc, name, shape, dt=F32):
    return _CachedGuard(lambda: nc.psum_tensor(name, shape, dt, allow_name_mangling=True))


def p0(nc, em, C, t, pr, mk, ident16_h, ident32_h, onesc_h, onesr_h, epsc_h):
    D, F = C.D, C.F
    em.op("sync", [], lambda: nc.sync.dma_start(out=pr["epsc"][:, :], in_=epsc_h.ap()), "dm", 16)
    mk["dm_eps"] = em.c["dm"]
    em.op("sync", [], lambda: nc.sync.dma_start(out=pr["ident16"][:, :], in_=ident16_h.ap()), "dm", 16)
    em.op("sync", [], lambda: nc.sync.dma_start(out=pr["ident32"][:, :], in_=ident32_h.ap()), "dm", 16)
    em.op("sync", [], lambda: nc.sync.dma_start(out=pr["onesc"][:, :], in_=onesc_h.ap()), "dm", 16)
    em.op("sync", [], lambda: nc.sync.dma_start(out=pr["onesr"][:, :], in_=onesr_h.ap()), "dm", 16)

    st = min(512, D)
    for r0 in range(0, D, st):
        em.op("gpsimd", [], lambda r0=r0, st=st: nc.gpsimd.dma_start(
            out=t["qkvb"][r0:r0 + st, :], in_=t["qkv_w"][r0:r0 + st, :]), "gw", 16)
    mk["gw_qkv"] = em.c["gw"]
    gw2 = []
    st = min(1024, F)
    for r0 in range(0, F, st):
        gw2.append(lambda r0=r0, st=st: nc.gpsimd.dma_start(
            out=t["w12b"][r0:r0 + st, :], in_=t["w12"][r0:r0 + st, :]))
    st = min(1024, D)
    for r0 in range(0, D, st):
        gw2.append(lambda r0=r0, st=st: nc.gpsimd.dma_start(
            out=t["w3b"][r0:r0 + st, :], in_=t["w3"][r0:r0 + st, :]))
    for r0 in range(0, D, st):
        gw2.append(lambda r0=r0, st=st: nc.gpsimd.dma_start(
            out=t["owb"][r0:r0 + st, :], in_=t["out_w"][r0:r0 + st, :]))
    mk["gw2list"] = gw2
    mk["gw2_total"] = 16 * len(gw2)
    mk["gxlist"] = [lambda: nc.gpsimd.dma_start(
        out=t["out"][:, :], in_=t["x"][0:C.OWN, :])]

    with _sb(nc, "brows", [128, 128], F32) as brows, \
         _ps(nc, "bps", [128, 128], F32) as bps:
        em.op("sync", [], lambda: nc.sync.dma_start(
            out=brows[0:3 * C.H, :], in_=t["qkv_b"].rearrange("(b p) -> b p", p=128)), "dm", 16)
        v = em.op("tensor", [("dm", em.c["dm"])],
                  lambda: nc.tensor.transpose(bps[:, 0:3 * C.H], brows[0:3 * C.H, :],
                                              pr["ident32"][0:3 * C.H, 0:3 * C.H]), "pe")
        v2 = em.op("vector", [("pe", v)],
                   lambda: nc.vector.tensor_copy(pr["qbcol"][:, :], bps[:, 0:3 * C.H]), "dve")
        v2 = em.op("vector", [("dve", v2)],
                   lambda: nc.vector.tensor_scalar_mul(
                       pr["qbcolS"][:, :], pr["qbcol"][:, 0:C.H], 1.0 / math.sqrt(128.0)), "dve")
        em.op("sync", [("dve", v2)], lambda: nc.sync.dma_start(
            out=brows[0:C.FC, :], in_=t["b12"].rearrange("(b p) -> b p", p=128)), "dm", 16)
        v = em.op("tensor", [("dm", em.c["dm"]), ("dve", v2)],
                  lambda: nc.tensor.transpose(bps[:, 0:C.FC], brows[0:C.FC, :],
                                              pr["ident32"][0:C.FC, 0:C.FC]), "pe")
        em.op("vector", [("pe", v)],
              lambda: nc.vector.tensor_copy(pr["b12col"][:, :], bps[:, 0:C.FC]), "dve")


def p1(nc, em, C, t, pr, mk):
    D, H = C.D, C.H
    qkvb_r = t["qkvb"].rearrange("(kc p) n -> p kc n", p=128)

    ngr = 1 if C.kvx else 2
    for g in range(ngr):
        R0 = g * C.GRP
        with contextlib.ExitStack() as gs:
            xnfm = gs.enter_context(_sb(nc, "xnfm", [128, C.DC * C.GRP], BF16))
            cosg = gs.enter_context(_sb(nc, "cosg", [64, C.GRP], BF16))
            sing = gs.enter_context(_sb(nc, "sing", [64, C.GRP], BF16))

            em.barrier()
            em.op("gpsimd", [], lambda: nc.gpsimd.dma_start(
                out=cosg[:, :], in_=t["cosk"][:, R0:R0 + C.GRP]), "dmg", 16)
            em.op("gpsimd", [], lambda: nc.gpsimd.dma_start(
                out=sing[:, :], in_=t["sink"][:, R0:R0 + C.GRP]), "dmg", 16)
            vmisc = em.c["dmg"]


            with contextlib.ExitStack() as asb:
                xr = asb.enter_context(_sb(nc, "xr", [128, 2 * D], F32))
                xn16 = asb.enter_context(_sb(nc, "xn16", [128, 2 * D], BF16))
                xsq = asb.enter_context(_sb(nc, "xsq", [128, D], BF16))
                stats = asb.enter_context(_sb(nc, "stats", [128, 8], F32))
                tp0 = asb.enter_context(_ps(nc, "tp0", [128, 1024], BF16))
                tp1 = asb.enter_context(_ps(nc, "tp1", [128, 1024], BF16))
                tp = [tp0, tp1]
                for T in range(C.NB):
                    sl = T % 2
                    gate = [("dve", mk.get(("xnstt", g, T - 2))),
                            ("act", mk.get(("xsq", g, T - 2)))]
                    mk[("xload", g, T)] = em.op(
                        "sync", gate, lambda T=T, sl=sl: nc.sync.dma_start(
                            out=xr[:, sl * D:(sl + 1) * D],
                            in_=t["x"][R0 + T * 128:R0 + T * 128 + 128, :]),
                        f"dx{sl}", 16)
                    dxv = [(f"dx{sl}", mk[("xload", g, T)])]
                    vsum = em.op("vector", dxv + [("act", mk.get(("negmu", g, T - 1)))],
                                 lambda sl=sl: nc.vector.reduce_sum(
                                     stats[:, 0:1], xr[:, sl * D:(sl + 1) * D], axis=AX.X), "dve")
                    mk[("xsq", g, T)] = em.op(
                        "scalar", dxv + [("act", mk.get(("xsq", g, T - 1))),
                                         ("dve", mk.get(("varr", g, T - 1)))],
                        lambda sl=sl: nc.scalar.activation(
                            xsq[:, :], xr[:, sl * D:(sl + 1) * D],
                            AF.Square, accum_out=stats[:, 1:2]), "act")
                    mk[("negmu", g, T)] = em.op(
                        "scalar", [("dve", vsum), ("dve", mk.get(("xnstt", g, T - 1)))],
                        lambda: nc.scalar.mul(stats[:, 2:3], stats[:, 0:1], -1.0 / D), "act")
                    vmu2 = em.op("scalar", [("act", mk[("negmu", g, T)]),
                                            ("dve", mk.get(("varr", g, T - 1)))],
                                 lambda: nc.scalar.activation(stats[:, 3:4], stats[:, 2:3],
                                                              AF.Square), "act")
                    mk[("varr", g, T)] = em.op(
                        "vector", [("act", vmu2), ("act", mk[("xsq", g, T)]),
                                   ("act", mk.get(("rstdo", g, T - 1)))],
                        lambda: nc.vector.scalar_tensor_tensor(
                            stats[:, 4:5], stats[:, 1:2], 1.0 / D, stats[:, 3:4],
                            ALU.mult, ALU.subtract), "dve")
                    vln = em.op("scalar", [("dve", mk[("varr", g, T)]), ("dm", mk["dm_eps"])],
                                lambda: nc.scalar.activation(stats[:, 5:6], stats[:, 4:5],
                                                             AF.Ln, bias=pr["epsc"][:, 0:1]), "act")
                    mk[("rstdo", g, T)] = em.op(
                        "scalar", [("act", vln), ("dve", mk.get(("xnstt", g, T - 1)))],
                        lambda: nc.scalar.activation(stats[:, 6:7], stats[:, 5:6],
                                                     AF.Exp, scale=-0.5), "act")
                    mk[("xnstt", g, T)] = em.op(
                        "vector", [("act", mk[("rstdo", g, T)]),
                                   ("pe", mk.get(("tpdone", g, T - 2)))],
                        lambda sl=sl: nc.vector.tensor_scalar(
                            xn16[:, sl * D:(sl + 1) * D], xr[:, sl * D:(sl + 1) * D],
                            stats[:, 2:3], stats[:, 6:7], ALU.add, ALU.mult), "dve")
                    GB = min(8, C.DC)
                    for kc in range(C.DC):
                        idx = T * C.DC + kc
                        bank = (idx // GB) % 2
                        pos = idx % GB
                        deps = [("dve", mk[("xnstt", g, T)]), ("dm", 16)]
                        if pos == 0:
                            deps.append(("dve", mk.get(("tpev", g, (idx // GB) - 2))))
                        last = (pos == GB - 1) or (idx == C.NB * C.DC - 1)
                        v = em.op("tensor", deps,
                                  lambda kc=kc, sl=sl, bank=bank, pos=pos: nc.tensor.transpose(
                                      tp[bank][:, pos * 128:(pos + 1) * 128],
                                      xn16[:, sl * D + kc * 128: sl * D + (kc + 1) * 128],
                                      pr["ident16"][:, :]),
                                  "pe" if last else None)
                        if last:
                            mk[("tpdone", g, T)] = v
                            b = idx // GB
                            npos = pos + 1
                            kc0 = (b * GB) % C.DC
                            T0 = (b * GB) // C.DC

                            def evac(b=b, bank=bank, npos=npos, kc0=kc0, T0=T0):
                                dst = xnfm[:, :].rearrange("p (kc u) -> p kc u", u=C.GRP)
                                dst = dst[:, kc0:kc0 + npos, T0 * 128:T0 * 128 + 128]
                                src = tp[bank][:, 0:npos * 128].rearrange(
                                    "p (kc u) -> p kc u", u=128)
                                return nc.vector.tensor_copy(dst, src)
                            mk[("tpev", g, b)] = em.op("vector", [("pe", v)], evac, "dve")
                mk[("xnfm", g)] = mk[("tpev", g, (C.NB * C.DC - 1) // min(8, C.DC))]

            if g == 0:
                for kc in range(C.DC):
                    em.op("scalar", [("dve", mk[("xnfm", 0)])],
                          lambda kc=kc: nc.scalar.dma_start(
                              out=t["xnst"][kc], in_=xnfm[:, kc * C.GRP:(kc + 1) * C.GRP]),
                          "sxn", 16)
                mk["sxn"] = em.c["sxn"]

            em.barrier()
            with contextlib.ExitStack() as bsb:
                wsl = [bsb.enter_context(_sb(nc, f"wsl{j}", [128, C.DC * 256], BF16))
                       for j in range(2)]
                wf32 = [bsb.enter_context(_sb(nc, f"wf32{j}",
                        [128, max(1, C.DC // max(4, C.DC // 4)) * 256], F32)) for j in range(2)]
                riL = [bsb.enter_context(_sb(nc, f"riL{j}", [64, C.FT], F32))
                       for j in range(2)]
                riH = [bsb.enter_context(_sb(nc, f"riH{j}", [64, C.FT], F32))
                       for j in range(2)]
                ttb = bsb.enter_context(_sb(nc, "ttb", [64, 4 * C.FT], F32))
                o16L = [bsb.enter_context(_sb(nc, f"o16L{j}", [64, C.FT], BF16))
                        for j in range(2)]
                o16H = [bsb.enter_context(_sb(nc, f"o16H{j}", [64, C.FT], BF16))
                        for j in range(2)]
                o16v = [bsb.enter_context(_sb(nc, f"o16v{j}", [128, C.FT], BF16))
                        for j in range(2)]
                qkps = [bsb.enter_context(_ps(nc, f"qk{j}", [128, 512], F32))
                        for j in range(8)]

                blocks = []
                blocks += [("k", h) for h in range(H)]
                blocks += [("v", h) for h in range(H)]
                if g == 0:
                    blocks += [("q", h) for h in range(H)]
                nbat = len(blocks) // 2
                rnd = 0
                scnt = 0
                for bi in range(nbat):
                    kind, h0 = blocks[2 * bi]
                    base = {"q": 0, "k": D, "v": 2 * D}[kind]
                    col0 = base + h0 * 128
                    ssl = bi % 2
                    KG = min(4, C.DC)
                    NCAST = 12 if g == 0 and C.kvx else 0
                    if bi < NCAST:
                        gate = [("pe", mk.get(("slabfree", g, bi - 2)))]
                        qw_r = t["qkv_w"].rearrange("(kc p) n -> p kc n", p=128)
                        nhf = max(4, C.DC // 4)
                        hw = C.DC // nhf
                        for hf in range(nhf):
                            rr = hf % 2
                            dep2 = [("dve", mk.get(("bcast", g, bi, hf - 2)) if hf >= 2
                                     else mk.get(("bcast", g, bi - 1, nhf - 2 + hf)))]
                            em.op("sync", (gate if hf == 0 else []) + dep2,
                                  lambda col0=col0, hf=hf, hw=hw, rr=rr: nc.sync.dma_start(
                                      out=wf32[rr][:, :].rearrange("p (k n) -> p k n", n=256),
                                      in_=qw_r[:, hf * hw:(hf + 1) * hw, col0:col0 + 256]),
                                  f"wf{rr}", 16)
                            mk[("bcast", g, bi, hf)] = em.op(
                                "vector", [(f"wf{rr}", em.c[f"wf{rr}"])],
                                lambda ssl=ssl, hf=hf, hw=hw, rr=rr: nc.vector.tensor_copy(
                                    wsl[ssl][:, hf * hw * 256:(hf + 1) * hw * 256],
                                    wf32[rr][:, :]), "dve")
                        mk[("wsl", g, bi)] = 0
                        mk[("wslc", g, bi)] = ("dve", mk[("bcast", g, bi, nhf - 1)])
                    else:
                        gate = [("pe", mk.get(("slabfree", g, bi - 2))),
                                ("gw", mk["gw_qkv"])]
                        for kg in range(C.DC // KG):
                            em.op("sync", gate if kg == 0 else [],
                                  lambda col0=col0, kg=kg, ssl=ssl, KG=KG: nc.sync.dma_start(
                                      out=wsl[ssl][:, kg * KG * 256:(kg + 1) * KG * 256]
                                      .rearrange("p (k n) -> p k n", n=256),
                                      in_=qkvb_r[:, kg * KG:(kg + 1) * KG, col0:col0 + 256]),
                                  f"ws{ssl}", 16)
                        mk[("wsl", g, bi)] = em.c[f"ws{ssl}"]
                        mk[("wslc", g, bi)] = None
                    for dl in range(2):
                        for kc in range(C.DC):
                            for tau in range(C.TAU):
                                bank = dl * C.TAU + tau
                                deps = [(f"ws{ssl}", mk[("wsl", g, bi)]),
                                        mk.get(("wslc", g, bi)),
                                        ("dve", mk[("xnfm", g)])]
                                if kc == 0:
                                    deps.append(("act", mk.get(("riev", g, bi - 1, dl, tau))))
                                markit = (kc == C.DC - 1) or (
                                    dl == 1 and kc == C.DC - 1 and tau == C.TAU - 1)
                                v = em.op("tensor", deps,
                                          lambda bank=bank, kc=kc, tau=tau, ssl=ssl, dl=dl:
                                          nc.tensor.matmul(
                                              qkps[bank][:, 0:C.FT],
                                              wsl[ssl][:, kc * 256 + dl * 128:
                                                       kc * 256 + (dl + 1) * 128],
                                              xnfm[:, kc * C.GRP + tau * C.FT:
                                                   kc * C.GRP + (tau + 1) * C.FT],
                                              start=(kc == 0), stop=(kc == C.DC - 1)),
                                          "pe" if markit else None)
                                if kc == C.DC - 1:
                                    mk[("fill", g, bi, dl, tau)] = v
                        if dl == 1:
                            mk[("slabfree", g, bi)] = v
                    for dl in range(2):
                        h = h0 + dl
                        bcol = {"q": 0, "k": H, "v": 2 * H}[kind] + h
                        for tau in range(C.TAU):
                            bank = dl * C.TAU + tau
                            r = rnd % 2
                            ccol = (tau * C.FT, (tau + 1) * C.FT)
                            if kind == "v":
                                deps = [("pe", mk[("fill", g, bi, dl, tau)]),
                                        mk.get(("o16st", rnd - 2))]
                                va = em.op("scalar", deps,
                                           lambda bank=bank, r=r, bcol=bcol: nc.scalar.activation(
                                               o16v[r][:, :], qkps[bank][:, 0:C.FT], AF.Identity,
                                               bias=pr["qbcol"][:, bcol:bcol + 1]), "act")
                                mk[("riev", g, bi, dl, tau)] = va
                                dstap = (t["vin"][h] if C.kvx else t["vst"][h // 4, g, h % 4])[:, ccol[0]:ccol[1]]
                                so = f"sv{rnd % 2}"
                                sv = em.op("scalar", [("act", va)],
                                           lambda dstap=dstap, r=r: nc.scalar.dma_start(
                                               out=dstap, in_=o16v[r][:, :]), so, 16)
                                mk[("o16st", rnd)] = (so, sv)
                                mk.setdefault(("hst", kind, h), []).append((so, sv))
                                rnd += 1
                                HCG = min(4, H)
                                if C.kvx and tau == C.TAU - 1 and h % HCG == HCG - 1:
                                    grp = [[2 * b, 2 * b + 1] for b in range(C.ncc // 2)]
                                    h0g = h - HCG + 1
                                    deps = []
                                    for hh in range(h0g, h + 1):
                                        deps += mk[("hst", "v", hh)]
                                    em.op("gpsimd", deps,
                                          lambda h0g=h0g, HCG=HCG, grp=grp:
                                          nc.gpsimd.collective_compute(
                                              "AllGather", ALU.bypass,
                                              replica_groups=grp,
                                              ins=[t["vin"][h0g:h0g + HCG].rearrange(
                                                  "h p u -> (h p) u").opt()],
                                              outs=[t["vst"][h0g // HCG].rearrange(
                                                  "g h p u -> (g h p) u").opt()]),
                                          "cc", 1)
                                    for hh in range(h0g, h + 1):
                                        mk[("cch", hh)] = em.c["cc"]
                                continue
                            deps = [("pe", mk[("fill", g, bi, dl, tau)]),
                                    ("dve", mk.get(("rope", rnd - 2))),
                                    ("pool", mk.get(("prope", rnd - 2)))]
                            if kind == "q":
                                bap, bsc = pr["qbcolS"], C.scale
                                bq = h
                            else:
                                bap, bsc = pr["qbcol"], 1.0
                                bq = bcol
                            va = em.op("scalar", deps,
                                       lambda bank=bank, r=r, bq=bq, bap=bap, bsc=bsc:
                                       nc.scalar.activation(
                                           riL[r][:, :], qkps[bank][0:64, 0:C.FT], AF.Identity,
                                           bias=bap[0:64, bq:bq + 1], scale=bsc), "act")
                            va = em.op("scalar", [],
                                       lambda bank=bank, r=r, bq=bq, bap=bap, bsc=bsc:
                                       nc.scalar.activation(
                                           riH[r][:, :], qkps[bank][64:128, 0:C.FT], AF.Identity,
                                           bias=bap[64:128, bq:bq + 1], scale=bsc), "act")
                            mk[("riev", g, bi, dl, tau)] = va
                            sc = C.scale if kind == "q" else 1.0
                            arid = ("act", va)
                            cosu = cosg
                            sinu = sing
                            tdep = ("dmg", vmisc)
                            e1 = em.op("vector",
                                       [arid, tdep, ("dve", mk.get(("rope", rnd - 1)))],
                                       lambda r=r, cosu=cosu, ccol=ccol: nc.vector.tensor_mul(
                                           ttb[:, 0:C.FT], riL[r][:, :],
                                           cosu[:, ccol[0]:ccol[1]]), "dve")
                            e2 = em.op("vector", [arid, ("dve", e1)],
                                       lambda r=r, sinu=sinu, ccol=ccol: nc.vector.tensor_mul(
                                           ttb[:, C.FT:2 * C.FT], riH[r][:, :],
                                           sinu[:, ccol[0]:ccol[1]]), "dve")
                            e3 = em.op("vector", [("dve", e2), mk.get(("o16st", rnd - 2))],
                                       lambda r=r: nc.vector.tensor_sub(
                                           o16L[r][:, :], ttb[:, 0:C.FT],
                                           ttb[:, C.FT:2 * C.FT]), "dve")
                            mk[("rope", rnd)] = e3
                            p1_ = em.op("gpsimd",
                                        [arid, tdep, ("pool", mk.get(("prope", rnd - 1)))],
                                        lambda r=r, cosu=cosu, ccol=ccol: nc.gpsimd.tensor_mul(
                                            ttb[:, 2 * C.FT:3 * C.FT], riH[r][:, :],
                                            cosu[:, ccol[0]:ccol[1]]), "pool")
                            p2_ = em.op("gpsimd", [arid, ("pool", p1_)],
                                        lambda r=r, sinu=sinu, ccol=ccol: nc.gpsimd.tensor_mul(
                                            ttb[:, 3 * C.FT:4 * C.FT], riL[r][:, :],
                                            sinu[:, ccol[0]:ccol[1]]), "pool")
                            p3_ = em.op("gpsimd", [("pool", p2_), mk.get(("o16st", rnd - 2))],
                                        lambda r=r: nc.gpsimd.tensor_add(
                                            o16H[r][:, :], ttb[:, 2 * C.FT:3 * C.FT],
                                            ttb[:, 3 * C.FT:4 * C.FT]), "pool")
                            mk[("prope", rnd)] = p3_
                            if kind == "q":
                                dstap = t["qst"][h]
                                so = f"sq{rnd % 2}"
                            else:
                                dstap = t["kin"][h] if C.kvx else t["kst"][h // 4, g, h % 4]
                                so = f"sk{rnd % 2}"
                            em.op("scalar", [("dve", e3)],
                                  lambda dstap=dstap, r=r, ccol=ccol: nc.scalar.dma_start(
                                      out=dstap[0:64, ccol[0]:ccol[1]], in_=o16L[r][:, :]), so, 16)
                            sv = em.op("scalar", [("pool", p3_)],
                                       lambda dstap=dstap, r=r, ccol=ccol: nc.scalar.dma_start(
                                           out=dstap[64:128, ccol[0]:ccol[1]], in_=o16H[r][:, :]),
                                       so, 16)
                            mk[("o16st", rnd)] = (so, sv)
                            if kind == "k":
                                mk.setdefault(("hst", "k", h), []).append((so, sv))
                            rnd += 1
                            HCG = min(4, H)
                            if C.kvx and kind == "k" and tau == C.TAU - 1 \
                                    and h % HCG == HCG - 1:
                                grp = [[2 * b, 2 * b + 1] for b in range(C.ncc // 2)]
                                h0g = h - HCG + 1
                                deps = []
                                for hh in range(h0g, h + 1):
                                    deps += mk[("hst", "k", hh)]
                                em.op("gpsimd", deps,
                                      lambda h0g=h0g, HCG=HCG, grp=grp:
                                      nc.gpsimd.collective_compute(
                                          "AllGather", ALU.bypass,
                                          replica_groups=grp,
                                          ins=[t["kin"][h0g:h0g + HCG].rearrange(
                                              "h p u -> (h p) u").opt()],
                                          outs=[t["kst"][h0g // HCG].rearrange(
                                              "g h p u -> (g h p) u").opt()]),
                                      "cc", 1)


def p2(nc, em, C, t, pr, mk):
    H, S, TT = C.H, C.S, C.TT
    while mk.get("gw2list"):
        em.op("gpsimd", [], mk["gw2list"].pop(0), "gw2", 16)
    while mk.get("gxlist"):
        em.op("gpsimd", [], mk["gxlist"].pop(0), "gx", 16)
    em.barrier(skip=("cc", "gw2", "gw", "gx", "sxn"))
    with contextlib.ExitStack() as es:
        kh = [es.enter_context(_sb(nc, f"kh{j}", [128, S], BF16)) for j in range(2)]
        vh = [es.enter_context(_sb(nc, f"vh{j}", [128, S], BF16)) for j in range(2)]
        qh = [es.enter_context(_sb(nc, f"qh{j}", [128, C.OWN], BF16)) for j in range(2)]
        vtm = [es.enter_context(_sb(nc, f"vtm{j}", [128, S], BF16)) for j in range(2)]
        mA = es.enter_context(_sb(nc, "mA", [128, C.CPT * TT], BF16))
        mB = es.enter_context(_sb(nc, "mB", [128, C.CPT * TT], BF16))
        ex = [es.enter_context(_sb(nc, f"ex{j}", [128, TT], BF16)) for j in range(3)]
        rec = es.enter_context(_sb(nc, "rec", [1, 4 * TT], F32))
        rb = es.enter_context(_sb(nc, "rb", [128, TT], F32))
        cx = [es.enter_context(_sb(nc, f"cx{j}", [128, C.OWN], BF16)) for j in range(2)]
        ctxp = [es.enter_context(_ps(nc, f"ctx{s}", [128, 512], F32)) for s in range(4)]
        scp = [es.enter_context(_ps(nc, f"scp{j}", [128, 512], F32)) for j in range(2)]
        dnp = es.enter_context(_ps(nc, "dnp", [128, 512], F32))
        vtp = es.enter_context(_ps(nc, "vtp", [128, 1024], BF16))

        em.op("sync", [], lambda: nc.sync.dma_start(
            out=mA[:, :].rearrange("p (d u) -> p d u", u=TT),
            in_=t["maskA"].rearrange("d p u -> p d u")), "dm", 16)
        em.op("sync", [], lambda: nc.sync.dma_start(
            out=mB[:, :].rearrange("p (d u) -> p d u", u=TT),
            in_=t["maskB"].rearrange("d p u -> p d u")), "dm", 16)
        vmask = em.c["dm"]

        seq = []
        for sec in range(2):
            for c in range(C.SEC):
                for s in range(c // C.CPT, 4):
                    seq.append((sec, c, s))
        nidx = len(seq)
        first = {}
        last = {}
        for i, (sec, c, s) in enumerate(seq):
            first.setdefault(s, i)
            last[s] = i
        NR8 = C.NCH // 8

        def loads(h):
            sl = h % 2
            gate = [("pe", mk.get(("pv", h - 2, nidx - 1)))]
            if C.kvx:
                gate = gate + [("cc", mk.get(("cch", h), em.c["cc"]))]
            HCG = min(4, C.H)
            ksrc = t["kst"][h // HCG, :, h % HCG].rearrange("g p u -> p g u")
            mk[("khl", h)] = em.op("sync", gate, lambda ksrc=ksrc, sl=sl: nc.sync.dma_start(
                out=kh[sl][:, :].rearrange("p (g u) -> p g u", g=2), in_=ksrc), f"dk{sl}", 16)
            vgate = [("pe", mk.get(("vtpdone", h - 2, NR8 - 1)))]
            if C.kvx:
                vgate = vgate + [("cc", mk.get(("cch", h), em.c["cc"]))]
            vsrc = t["vst"][h // HCG, :, h % HCG].rearrange("g p u -> p g u")
            mk[("vhl", h)] = em.op("sync", vgate, lambda vsrc=vsrc, sl=sl: nc.sync.dma_start(
                out=vh[sl][:, :].rearrange("p (g u) -> p g u", g=2), in_=vsrc), f"dv{sl}", 16)
            mk[("qhl", h)] = em.op("sync", gate, lambda h=h, sl=sl: nc.sync.dma_start(
                out=qh[sl][:, :], in_=t["qst"][h]), f"du{sl}", 16)

        def vtrans_round(h, rb8):
            sl = h % 2
            for j in range(8):
                cidx = rb8 * 8 + j
                deps = [(f"dv{sl}", mk[("vhl", h)]), ("dm", 16)]
                if j == 0:
                    if rb8 == 0:
                        deps.append(("dve", mk.get(("vtev", h - 1, NR8 - 1))))
                        deps.append(("pe", mk.get(("pv", h - 2, nidx - 1))))
                    else:
                        deps.append(("dve", mk.get(("vtev", h, rb8 - 1))))
                v = em.op("tensor", deps,
                          lambda cidx=cidx, j=j, sl=sl: nc.tensor.transpose(
                              vtp[:, j * 128:(j + 1) * 128],
                              vh[sl][:, cidx * 128:(cidx + 1) * 128],
                              pr["ident16"][:, :]),
                          "pe" if j == 7 else None)
            mk[("vtpdone", h, rb8)] = v
            mk[("vtev", h, rb8)] = em.op(
                "vector", [("pe", v)], lambda rb8=rb8, sl=sl: nc.vector.tensor_copy(
                    vtm[sl][:, rb8 * 1024:(rb8 + 1) * 1024], vtp[:, :]), "dve")

        def pvpair(h, i, sl):
            sec, c, s = seq[i]
            kcol = sec * C.OWN + c * 128
            st = (i == first[s])
            sp = (i == last[s])
            pvdeps = [("act", mk[("exp", h, i)]), ("dve", mk.get(("vtev", h, NR8 - 1)))]
            if st:
                pvdeps.append(("dve", mk.get(("cxm", h - 1, s))))
            em.op("tensor", pvdeps, lambda s=s, kcol=kcol, i=i, sl=sl: nc.tensor.matmul(
                ctxp[s][:, 0:TT], vtm[sl][:, kcol:kcol + 128], ex[(h * nidx + i) % 3][:, :],
                start=st, stop=sp), None)
            dndeps = []
            if st:
                dndeps.append(("dve", mk.get(("rbcp", h - 1, 3))))
            mk[("pv", h, i)] = em.op(
                "tensor", dndeps, lambda s=s, i=i, st=st, sp=sp: nc.tensor.matmul(
                    dnp[32 * s:32 * s + 1, 0:TT], pr["onesc"][:, :], ex[(h * nidx + i) % 3][:, :],
                    start=st, stop=sp,
                    tile_position=(0, 32 * s)), "pe")

        trigger = {16 + r * 16: r for r in range(NR8)} if nidx >= 16 * NR8 + 16 else             {4 + r * 4: r for r in range(NR8)}
        for h in range(H):
            sl = h % 2
            if h == 0:
                loads(0)
                for r in range(NR8):
                    vtrans_round(0, r)
            if h + 1 < H:
                loads(h + 1)
            for i in range(nidx):
                sec, c, s = seq[i]
                gi = h * nidx + i
                sb = scp[gi % 2]
                kcol = sec * C.OWN + c * 128
                diag = (c // C.CPT == s)
                d = c % C.CPT
                deps = [(f"dk{sl}", mk[("khl", h)]), (f"du{sl}", mk[("qhl", h)]),
                        ("dm", vmask),
                        ("act", mk.get(("exp", h, i - 2), mk.get(("exp", h - 1, nidx - 1))))]
                v = em.op("tensor", deps, lambda sb=sb, kcol=kcol, s=s, diag=diag, sl=sl:
                          nc.tensor.matmul(
                              sb[:, 0:TT], kh[sl][:, kcol:kcol + 128],
                              qh[sl][:, s * TT:(s + 1) * TT],
                              start=True, stop=not diag), "pe" if not diag else None)
                if diag:
                    mm = mA if sec == 0 else mB
                    v = em.op("tensor", [], lambda sb=sb, mm=mm, d=d: nc.tensor.matmul(
                        sb[:, 0:TT], pr["ident16"][:, :], mm[:, d * TT:(d + 1) * TT],
                        start=False, stop=True), "pe")
                mk[("score", h, i)] = v
                pgi = gi - 3
                ph, pi = divmod(pgi, nidx) if pgi >= 0 else (None, None)
                deps = [("pe", v)]
                if ph is not None:
                    deps.append(("pe", mk.get(("pv", ph, pi))))
                mk[("exp", h, i)] = em.op("scalar", deps, lambda sb=sb, gi=gi:
                                          nc.scalar.activation(
                                              ex[gi % 3][:, :], sb[:, 0:TT], AF.Exp), "act")
                if i >= 1:
                    pvpair(h, i - 1, sl)
                if h + 1 < H and i in trigger:
                    vtrans_round(h + 1, trigger[i])
            pvpair(h, nidx - 1, sl)

            for s in range(4):
                mk[("recip", h, s)] = em.op(
                    "vector", [("pe", mk[("pv", h, last[s])]),
                               ("pe", mk.get(("bcmm", h - 1, s)))],
                    lambda s=s: nc.vector.reciprocal(
                        rec[0:1, s * TT:(s + 1) * TT],
                        dnp[32 * s:32 * s + 1, 0:TT]), "dve")
            for s in range(4):
                vb = em.op("tensor", [("dve", mk[("recip", h, 3)]),
                                      ("dve", mk.get(("rbcp", h, s - 1)))],
                           lambda s=s: nc.tensor.matmul(
                               dnp[:, 0:TT], pr["onesr"][:, :],
                               rec[0:1, s * TT:(s + 1) * TT],
                               start=True, stop=True), "pe")
                mk[("bcmm", h, s)] = vb
                vc = em.op("vector", [("pe", vb), ("dve", mk.get(("cxm", h, s - 1)))],
                           lambda: nc.vector.tensor_copy(rb[:, :], dnp[:, 0:TT]), "dve")
                mk[("rbcp", h, s)] = vc
                scs = f"sc{h % 2}"
                mk[("cxm", h, s)] = em.op(
                    "vector", [("dve", vc), (scs, mk.get((scs + "cnt", h - 2), 0))],
                    lambda h=h, s=s: nc.vector.tensor_mul(
                        cx[h % 2][:, s * TT:(s + 1) * TT], ctxp[s][:, 0:TT], rb[:, :]), "dve")
            mk[(f"sc{h % 2}cnt", h)] = em.op(
                "scalar", [("dve", mk[("cxm", h, 3)])],
                lambda h=h: nc.scalar.dma_start(out=t["cxst"][h], in_=cx[h % 2][:, :]),
                f"sc{h % 2}", 16)


def p4(nc, em, C, t, pr, mk):
    D, H, M = C.D, C.H, C.M
    em.barrier()
    with contextlib.ExitStack() as es:
        bsb = es.enter_context(_sb(nc, "bsb", [128, D], BF16))
        x4 = es.enter_context(_sb(nc, "x4", [128, C.MC * C.GP4], BF16))

        with _sb(nc, "obr", [1, D], F32) as obr, \
             _sb(nc, "b3r", [1, D], F32) as b3r, \
             _ps(nc, "bps2", [128, 512], F32) as bps2:
            em.op("sync", [], lambda: nc.sync.dma_start(
                out=obr[:, :], in_=t["out_b"].unsqueeze(0)), "dm", 16)
            em.op("sync", [], lambda: nc.sync.dma_start(
                out=b3r[:, :], in_=t["b3"].unsqueeze(0)), "dm", 16)
            va = em.op("vector", [("dm", em.c["dm"])],
                       lambda: nc.vector.tensor_add(obr[:, :], obr[:, :], b3r[:, :]), "dve")
            vprev = None
            for j in range(C.DT):
                v = em.op("tensor", [("dve", va), ("dve", vprev)],
                          lambda j=j: nc.tensor.matmul(bps2[:, 0:512], pr["onesr"][:, :],
                                                       obr[:, j * 512:(j + 1) * 512],
                                                       start=True, stop=True), "pe")
                vprev = em.op("vector", [("pe", v)],
                              lambda j=j: nc.vector.tensor_copy(
                                  bsb[:, j * 512:(j + 1) * 512], bps2[:, 0:512]), "dve")

        for g4 in range(C.NG4):
            Q0 = g4 * C.GP4
            nblk = min(4, C.MC)
            ni4 = C.MC // nblk
            with contextlib.ExitStack() as ws:
                xnp4 = ws.enter_context(_sb(nc, "xnp4", [128, C.DC * C.GP4], BF16))
                w1s = [ws.enter_context(_sb(nc, "w1s0", [128, C.DC * 512], BF16))]
                w2s = [ws.enter_context(_sb(nc, "w2s0", [128, C.DC * 512], BF16))]
                s1b = ws.enter_context(_sb(nc, "s1b", [128, C.GP4], F32))
                fps = [ws.enter_context(_ps(nc, f"fp{j}", [128, 512], F32))
                       for j in range(8)]
                em.barrier()
                for kc in range(C.DC):
                    em.op("sync", [("sxn", mk.get("sxn", 0))],
                          lambda kc=kc, Q0=Q0: nc.sync.dma_start(
                              out=xnp4[:, kc * C.GP4:(kc + 1) * C.GP4],
                              in_=t["xnst"][kc][:, Q0:Q0 + C.GP4]), "dxn", 16)
                mk[("xnp4", g4)] = em.c["dxn"]
                for i4 in range(ni4):
                    ssl = 0
                    GK = min(8, C.DC)
                    for kg in range(C.DC // GK):
                        p_ = kg % 2
                        kcl = kg * GK + GK - 1
                        gate = [("pe", mk.get(("w12kc", g4, i4 - 1, kcl))),
                                (f"dw{p_}", mk.get(("dwq", g4, i4, kg - 2),
                                                   mk.get(("dwqp", g4, i4 - 1, p_))))]
                        if i4 == 0:
                            gate.append(("gw2", mk["gw2_total"]))
                        for kc in range(kg * GK, (kg + 1) * GK):
                            for half, wdst in ((0, w1s), (1, w2s)):
                                r0 = half * M + i4 * 512
                                em.op("sync", gate if (half == 0 and kc == kg * GK) else [],
                                      lambda wdst=wdst, kc=kc, r0=r0: nc.sync.dma_start(
                                          out=wdst[0][:, kc * 512:(kc + 1) * 512],
                                          in_=t["w12b"][r0:r0 + 512, kc * 128:(kc + 1) * 128],
                                          transpose=True),
                                      f"dw{p_}", 16)
                        mk[("dwq", g4, i4, kg)] = em.c[f"dw{p_}"]
                        mk[("dwqp", g4, i4, p_)] = em.c[f"dw{p_}"]
                        for kc in range(kg * GK, (kg + 1) * GK):
                            mk[("wslkc", g4, i4, kc)] = (f"dw{p_}", em.c[f"dw{p_}"])
                    for ih in range(nblk // 2):
                        ib0 = ih * 2
                        for kc in range(C.DC):
                            for ib2 in range(2):
                                ib = ib0 + ib2
                                i = i4 * nblk + ib
                                for half in range(2):
                                    wsl_ = w1s if half == 0 else w2s
                                    for tau in range(C.T4):
                                        bank = ib2 * 4 + half * 2 + tau
                                        deps = [mk[("wslkc", g4, i4, kc)],
                                                ("dxn", mk[("xnp4", g4)])]
                                        if kc == 0:
                                            deps += [("act", mk.get(("silu", g4, i - 2, tau))),
                                                     ("dve", mk.get(("x4ev", g4, i - 2, tau)))]
                                        markit = (kc == C.DC - 1) or (
                                            ih == nblk // 2 - 1 and ib2 == 1
                                            and half == 1 and tau == C.T4 - 1)
                                        v = em.op(
                                            "tensor", deps,
                                            lambda bank=bank, kc=kc, tau=tau, wsl_=wsl_, ib=ib:
                                            nc.tensor.matmul(
                                                fps[bank][:, 0:C.FT4],
                                                wsl_[0][:, kc * 512 + ib * 128:
                                                        kc * 512 + ib * 128 + 128],
                                                xnp4[:, kc * C.GP4 + tau * C.FT4:
                                                     kc * C.GP4 + (tau + 1) * C.FT4],
                                                start=(kc == 0), stop=(kc == C.DC - 1)),
                                            "pe" if markit else None)
                                        if ih == nblk // 2 - 1 and ib2 == 1 and half == 1 \
                                                and tau == C.T4 - 1:
                                            mk[("w12kc", g4, i4, kc)] = v
                                        if kc == C.DC - 1:
                                            mk[("x1fill" if half == 0 else "x2fill",
                                                g4, i4, ib, tau)] = v
                        for ib2 in range(2):
                            ib = ib0 + ib2
                            i = i4 * nblk + ib
                            for tau in range(C.T4):
                                b1 = ib2 * 4 + tau
                                b2 = ib2 * 4 + 2 + tau
                                vs = em.op("scalar", [("pe", mk[("x1fill", g4, i4, ib, tau)]),
                                                      ("dve", mk.get(("x4ev", g4, i - 1, tau)))],
                                           lambda b1=b1, tau=tau, i=i: nc.scalar.activation(
                                               s1b[:, tau * C.FT4:(tau + 1) * C.FT4],
                                               fps[b1][:, 0:C.FT4],
                                               AF.Silu, bias=pr["b12col"][:, i:i + 1]), "act")
                                mk[("silu", g4, i, tau)] = vs
                                deps = [("act", vs), ("pe", mk[("x2fill", g4, i4, ib, tau)]),
                                        ("pe", mk.get(("p4bdone", g4 - 1)))]
                                mk[("x4ev", g4, i, tau)] = em.op(
                                    "vector", deps,
                                    lambda b2=b2, i=i, tau=tau: nc.vector.scalar_tensor_tensor(
                                        x4[:, i * C.GP4 + tau * C.FT4:
                                           i * C.GP4 + (tau + 1) * C.FT4],
                                        fps[b2][:, 0:C.FT4],
                                        pr["b12col"][:, C.MC + i:C.MC + i + 1],
                                        s1b[:, tau * C.FT4:(tau + 1) * C.FT4],
                                        ALU.add, ALU.mult), "dve")
                mk[("w12don", g4)] = mk[("x4ev", g4, C.MC - 1, C.T4 - 1)]

            with contextlib.ExitStack() as bs:
                cxs = bs.enter_context(_sb(nc, "cxs", [128, C.DC * C.QSB], BF16))
                w3s = bs.enter_context(_sb(nc, "w3s", [128, C.MC * 512], BF16))
                ows = bs.enter_context(_sb(nc, "ows", [128, C.DC * 512], BF16))
                ot = [bs.enter_context(_sb(nc, f"ot{j}", [128, 512], F32))
                      for j in range(2)]
                ops_ = [bs.enter_context(_ps(nc, f"op{j}", [128, 512], F32))
                        for j in range(8)]
                em.barrier()
                ornd = 0
                hMC = max(1, C.MC // 2)
                hDC = max(1, C.DC // 2)
                for qs in range(C.NQS):
                    QQ = Q0 + qs * C.QSB
                    gate = [("pe", mk.get(("obank", g4, qs - 1, C.DT - 1, C.QB - 1)))]
                    if em.on("sync"):
                        em.w("sc0", em.c["sc0"])
                        em.w("sc1", em.c["sc1"])
                    for hh in range(H):
                        em.op("sync", gate if hh == 0 else [],
                              lambda hh=hh, QQ=QQ: nc.sync.dma_start(
                                  out=cxs[:, hh * C.QSB:(hh + 1) * C.QSB],
                                  in_=t["cxst"][hh][:, QQ:QQ + C.QSB]), "dc0", 16)
                    mk[("cxs", g4, qs)] = em.c["dc0"]
                    for j in range(C.DT):
                        GK3 = min(8, C.MC)
                        for kg in range(C.MC // GK3):
                            p_ = kg % 2
                            kcl = kg * GK3 + GK3 - 1
                            pk = (g4, qs, j - 1, kcl) if j > 0 else (g4, qs - 1, C.DT - 1, kcl)
                            gate2 = [("pe", mk.get(("w3kc",) + pk)),
                                     (f"dn{p_}", mk.get(("dnqp", g4, qs, j, p_),
                                                        mk.get(("dnqpl", p_)))),
                                     ("dve", mk[("w12don", g4)])]
                            for kc in range(kg * GK3, (kg + 1) * GK3):
                                em.op("sync", gate2 if kc == kg * GK3 else [],
                                      lambda kc=kc, j=j: nc.sync.dma_start(
                                          out=w3s[:, kc * 512:(kc + 1) * 512],
                                          in_=t["w3b"][j * 512:(j + 1) * 512,
                                                       kc * 128:(kc + 1) * 128],
                                          transpose=True), f"dn{p_}", 16)
                            mk[("dnqp", g4, qs, j, p_)] = em.c[f"dn{p_}"]
                            mk[("dnqpl", p_)] = em.c[f"dn{p_}"]
                            for kc in range(kg * GK3, (kg + 1) * GK3):
                                mk[("w3ld", g4, qs, j, kc)] = (f"dn{p_}", em.c[f"dn{p_}"])
                        GKo = min(8, C.DC)
                        for kg in range(C.DC // GKo):
                            p_ = kg % 2
                            kcl = kg * GKo + GKo - 1
                            pk = (g4, qs, j - 1, kcl) if j > 0 else (g4, qs - 1, C.DT - 1, kcl)
                            ogate = [("pe", mk.get(("owkc",) + pk)),
                                     (f"do{p_}", mk.get(("doqp", g4, qs, j, p_),
                                                        mk.get(("doqpl", p_))))]
                            for kc in range(kg * GKo, (kg + 1) * GKo):
                                em.op("sync", ogate if kc == kg * GKo else [],
                                      lambda kc=kc, j=j: nc.sync.dma_start(
                                          out=ows[:, kc * 512:(kc + 1) * 512],
                                          in_=t["owb"].rearrange("(kc p) n -> p kc n", p=128)
                                          [:, kc, j * 512:(j + 1) * 512]), f"do{p_}", 16)
                            mk[("doqp", g4, qs, j, p_)] = em.c[f"do{p_}"]
                            mk[("doqpl", p_)] = em.c[f"do{p_}"]
                            for kc in range(kg * GKo, (kg + 1) * GKo):
                                mk[("owld", g4, qs, j, kc)] = (f"do{p_}", em.c[f"do{p_}"])
                        for pi, wk in enumerate(("w3", "ow")):
                            nkc = C.MC if wk == "w3" else C.DC
                            for kc in range(nkc):
                                for qb in range(C.QB):
                                    bank = (j % 2) * 4 + qb
                                    if wk == "w3":
                                        deps = [mk[("w3ld", g4, qs, j, kc)]]
                                        if pi == 0 and kc == 0:
                                            pq, pj = (qs, j - 2) if j >= 2 else (qs - 1, C.DT + j - 2)
                                            deps.append(("dve", mk.get(("oev", g4, pq, pj, qb))))
                                        markit = (qb == C.QB - 1)
                                        v = em.op(
                                            "tensor", deps,
                                            lambda bank=bank, kc=kc, qb=qb, pi=pi, qs=qs:
                                            nc.tensor.matmul(
                                                ops_[bank][:, 0:512],
                                                x4[:, kc * C.GP4 + qs * C.QSB + qb * 128:
                                                   kc * C.GP4 + qs * C.QSB + qb * 128 + 128],
                                                w3s[:, kc * 512:(kc + 1) * 512],
                                                start=(pi == 0 and kc == 0), stop=False),
                                            "pe" if markit else None)
                                        if markit:
                                            mk[("w3kc", g4, qs, j, kc)] = v
                                    else:
                                        deps = [mk[("owld", g4, qs, j, kc)],
                                                ("dc0", mk[("cxs", g4, qs)])]
                                        lastmm = kc == nkc - 1
                                        markit = (qb == C.QB - 1) or lastmm
                                        v = em.op(
                                            "tensor", deps,
                                            lambda bank=bank, kc=kc, qb=qb, lastmm=lastmm:
                                            nc.tensor.matmul(
                                                ops_[bank][:, 0:512],
                                                cxs[:, kc * C.QSB + qb * 128:
                                                   kc * C.QSB + qb * 128 + 128],
                                                ows[:, kc * 512:(kc + 1) * 512],
                                                start=False, stop=lastmm),
                                            "pe" if markit else None)
                                        if qb == C.QB - 1:
                                            mk[("owkc", g4, qs, j, kc)] = v
                                        if lastmm:
                                            mk[("obank", g4, qs, j, qb)] = v
                        for qb in range(C.QB):
                            bank = (j % 2) * 4 + qb
                            r = ornd % 2
                            vt = em.op("vector",
                                       [("pe", mk[("obank", g4, qs, j, qb)]),
                                        (f"so{r}", mk.get((f"so{r}cnt", ornd - 2), 0))],
                                       lambda bank=bank, j=j, r=r: nc.vector.scalar_tensor_tensor(
                                           ot[r][:, :], ops_[bank][:, 0:512], 1.0,
                                           bsb[:, j * 512:(j + 1) * 512],
                                           ALU.mult, ALU.add), "dve")
                            mk[("oev", g4, qs, j, qb)] = vt
                            row0 = qs * C.QSB + qb * 128 + Q0
                            mk[(f"so{r}cnt", ornd)] = em.op(
                                "gpsimd", [("dve", vt), ("gx", 16)],
                                lambda row0=row0, j=j, r=r: nc.gpsimd.dma_start(
                                    out=t["out"][row0:row0 + 128, j * 512:(j + 1) * 512],
                                    in_=ot[r][:, :], accum_op=ALU.add), f"so{r}", 16)
                            ornd += 1
                mk[("p4bdone", g4)] = mk[("obank", g4, C.NQS - 1, C.DT - 1, C.QB - 1)]


def build_nc(C: Cfg, debug=False):
    nc = bass.Bass("TRN2", target_bir_lowering=False)
    D, S, H, F, M = C.D, C.S, C.H, C.F, C.M

    t = {}
    t["x"] = nc.dram_tensor("x", [S, D], F32, kind="ExternalInput").ap()
    t["qkv_w"] = nc.dram_tensor("qkv_w", [D, 3 * D], F32, kind="ExternalInput").ap()
    t["qkv_b"] = nc.dram_tensor("qkv_b", [3 * D], F32, kind="ExternalInput").ap()
    t["out_w"] = nc.dram_tensor("out_w", [D, D], F32, kind="ExternalInput").ap()
    t["out_b"] = nc.dram_tensor("out_b", [D], F32, kind="ExternalInput").ap()
    t["w12"] = nc.dram_tensor("w12", [F, D], F32, kind="ExternalInput").ap()
    t["b12"] = nc.dram_tensor("b12", [F], F32, kind="ExternalInput").ap()
    t["w3"] = nc.dram_tensor("w3", [D, M], F32, kind="ExternalInput").ap()
    t["b3"] = nc.dram_tensor("b3", [D], F32, kind="ExternalInput").ap()
    t["cosk"] = nc.dram_tensor("cosk", [64, S], F32, kind="ExternalInput").ap()
    t["sink"] = nc.dram_tensor("sink", [64, S], F32, kind="ExternalInput").ap()
    t["maskA"] = nc.dram_tensor("maskA", [C.CPT, 128, C.TT], BF16, kind="ExternalInput").ap()
    t["maskB"] = nc.dram_tensor("maskB", [C.CPT, 128, C.TT], BF16, kind="ExternalInput").ap()
    t["out"] = nc.dram_tensor("out", [C.OWN, D], F32, kind="ExternalOutput").ap()

    ik = "ExternalOutput" if debug else "Internal"
    t["qkvb"] = nc.dram_tensor("qkvb", [D, 3 * D], BF16, kind="Internal").ap()
    t["w12b"] = nc.dram_tensor("w12b", [F, D], BF16, kind="Internal").ap()
    t["w3b"] = nc.dram_tensor("w3b", [D, M], BF16, kind="Internal").ap()
    t["owb"] = nc.dram_tensor("owb", [D, D], BF16, kind="Internal").ap()
    HCG = min(4, H)
    t["kst"] = nc.dram_tensor("kst", [H // HCG, 2, HCG, 128, C.OWN], BF16, kind=ik).ap()
    t["vst"] = nc.dram_tensor("vst", [H // HCG, 2, HCG, 128, C.OWN], BF16, kind=ik).ap()
    t["kin"] = nc.dram_tensor("kin", [H, 128, C.OWN], BF16, kind="Internal").ap()
    t["vin"] = nc.dram_tensor("vin", [H, 128, C.OWN], BF16, kind="Internal").ap()
    t["qst"] = nc.dram_tensor("qst", [H, 128, C.OWN], BF16, kind=ik).ap()
    t["xnst"] = nc.dram_tensor("xnst", [C.DC, 128, C.OWN], BF16, kind=ik).ap()
    t["cxst"] = nc.dram_tensor("cxst", [H, 128, C.OWN], BF16, kind=ik).ap()

    ident16_h = nc.inline_tensor(np.eye(128, dtype=ml_dtypes.bfloat16), "c_ident16")
    ident32_h = nc.inline_tensor(np.eye(128, dtype=np.float32), "c_ident32")
    onesc_h = nc.inline_tensor(np.ones((128, 1), dtype=ml_dtypes.bfloat16), "c_onesc")
    onesr_h = nc.inline_tensor(np.ones((1, 128), dtype=np.float32), "c_onesr")
    epsc_h = nc.inline_tensor(np.full((128, 1), EPS, dtype=np.float32), "c_epsc")

    global _EM
    with contextlib.ExitStack() as top:
        em = Em(nc, SEMNAMES, top)
        _EM = em
        pr = {}
        pr["ident16"] = top.enter_context(nc.sbuf_tensor("ident16", [128, 128], BF16))
        pr["ident32"] = top.enter_context(nc.sbuf_tensor("ident32", [128, 128], F32))
        pr["onesc"] = top.enter_context(nc.sbuf_tensor("onesc", [128, 1], BF16))
        pr["onesr"] = top.enter_context(nc.sbuf_tensor("onesr", [1, 128], F32))
        pr["qbcol"] = top.enter_context(nc.sbuf_tensor("qbcol", [128, 3 * H], F32))
        pr["qbcolS"] = top.enter_context(nc.sbuf_tensor("qbcolS", [128, H], F32))
        pr["b12col"] = top.enter_context(nc.sbuf_tensor("b12col", [128, C.FC], F32))
        pr["epsc"] = top.enter_context(nc.sbuf_tensor("epsc", [128, 1], F32))

        def emit_all():
            mk = {}
            p0(nc, em, C, t, pr, mk, ident16_h, ident32_h, onesc_h, onesr_h, epsc_h)
            p1(nc, em, C, t, pr, mk)
            p2(nc, em, C, t, pr, mk)
            p4(nc, em, C, t, pr, mk)
            if em.on("gpsimd"):
                em.w("so0", em.c["so0"])
                em.w("so1", em.c["so1"])

        with nc.Block() as blk:
            for eng in ENGS:
                def body(E, eng=eng):
                    em.begin(eng, E)
                    emit_all()
                getattr(blk, eng)(body)
    return nc



def _rope_tables(S):
    half = 64
    inv = 1.0 / (ROPE_BASE ** (np.arange(half, dtype=np.float64) / half))
    ang = np.arange(S, dtype=np.float64)[:, None] * inv[None, :]
    return np.cos(ang).T.astype(np.float32), np.sin(ang).T.astype(np.float32)


def _masks(C, t):
    mA = np.zeros((C.CPT, 128, C.TT), dtype=np.float32)
    mB = np.zeros((C.CPT, 128, C.TT), dtype=np.float32)
    ii = np.arange(128)[:, None]
    jj = np.arange(C.TT)[None, :]
    dmask = np.stack([np.where(jj - ii >= 128 * d, 0.0, MASKVAL) for d in range(C.CPT)])
    if C.kvx:
        if t == 0:
            mA[:] = dmask
            mB[:] = MASKVAL
        else:
            mA[:] = 0.0
            mB[:] = dmask
    else:
        mA[:] = dmask
        mB[:] = 0.0 if t == 1 else MASKVAL
    return (mA.astype(ml_dtypes.bfloat16), mB.astype(ml_dtypes.bfloat16))


def stage_core(C, c, x_full, cos_t, sin_t):
    t = c % 2
    b = c // 2
    own = [2 * s + t for s in range(4)]
    par = [2 * s + 1 - t for s in range(4)]
    TT2 = C.S // 8
    rows = np.concatenate([np.arange(p * TT2, (p + 1) * TT2) for p in own + par])
    mA, mB = _masks(C, t)
    return {
        "x": np.ascontiguousarray(x_full[b][rows]),
        "cosk": np.ascontiguousarray(cos_t[:, rows]),
        "sink": np.ascontiguousarray(sin_t[:, rows]),
        "maskA": mA,
        "maskB": mB,
    }, rows


def kernel(L0_attn_hidden_states, L0_attn_qkv_w, L0_attn_qkv_b, L0_attn_out_w,
           L0_attn_out_b, L0_ffn_w12, L0_ffn_b12, L0_ffn_w3, L0_ffn_b3):
    C = CFG_FULL
    x = np.asarray(L0_attn_hidden_states, dtype=np.float32)
    cos_t, sin_t = _rope_tables(C.S)
    shared = {
        "qkv_w": np.asarray(L0_attn_qkv_w, dtype=np.float32),
        "qkv_b": np.asarray(L0_attn_qkv_b, dtype=np.float32),
        "out_w": np.asarray(L0_attn_out_w, dtype=np.float32),
        "out_b": np.asarray(L0_attn_out_b, dtype=np.float32),
        "w12": np.asarray(L0_ffn_w12, dtype=np.float32),
        "b12": np.asarray(L0_ffn_b12, dtype=np.float32),
        "w3": np.asarray(L0_ffn_w3, dtype=np.float32),
        "b3": np.asarray(L0_ffn_b3, dtype=np.float32),
    }
    nc = build_nc(C)
    in_maps = []
    rowmaps = []
    for c in range(8):
        m, rows = stage_core(C, c, x, cos_t, sin_t)
        m.update(shared)
        in_maps.append(m)
        rowmaps.append(rows)
    res = run_bass_kernel_spmd(nc, in_maps, core_ids=list(range(8)))
    out = np.empty((4, C.S, C.D), dtype=np.float32)
    for c in range(8):
        b = c // 2
        out[b][rowmaps[c][:C.OWN]] = res.results[c]["out"]
    return out


# version 17 (final)
# speedup vs baseline: 1.1665x; 1.1665x over previous
import contextlib
import math
import numpy as np
import ml_dtypes
import concourse.bass as bass
import concourse.mybir as mybir
from concourse.bass_utils import run_bass_kernel_spmd

F32 = mybir.dt.float32
BF16 = mybir.dt.bfloat16
AF = mybir.ActivationFunctionType
ALU = mybir.AluOpType
AX = mybir.AxisListType

MASKVAL = -66504.0
EPS = 1e-5
ROPE_BASE = 10000.0


class Cfg:
    def __init__(self, D, S, H, F):
        self.D, self.S, self.H, self.F = D, S, H, F
        self.HD = 128
        assert D == H * self.HD
        self.M = F // 2
        self.DC = D // 128
        self.MC = self.M // 128
        self.FC = F // 128
        self.TT = S // 8
        assert self.TT <= 512
        self.CPT = self.TT // 128
        self.OWN = S // 2
        self.GRP = self.OWN
        self.NB = self.GRP // 128
        self.FT = min(512, self.GRP)
        self.TAU = self.GRP // self.FT
        self.SEC = self.OWN // 128
        self.NCH = self.S // 128
        self.DT = D // 512
        self.GP4 = min(1024, self.OWN)
        self.NG4 = self.OWN // self.GP4
        self.FT4 = min(512, self.GP4)
        self.T4 = self.GP4 // self.FT4
        self.QSB = min(512, self.GP4)
        self.NQS = self.GP4 // self.QSB
        self.QB = self.QSB // 128
        self.scale = 1.0 / math.sqrt(128.0)
        self.kvx = True
        self.ncc = 8


CFG_FULL = Cfg(4096, 4096, 32, 8192)
CFG_SIM = Cfg(512, 1024, 4, 1024)
CFG_MID = Cfg(1024, 2048, 8, 2048)

ENGS = ["sync", "gpsimd", "tensor", "scalar", "vector"]


class Em:
    def __init__(self, nc, semnames, stack):
        self.nc = nc
        self.semnames = list(semnames)
        self.sems = {n: stack.enter_context(nc.semaphore(n)) for n in semnames}

    def begin(self, eng, E):
        self.eng = eng
        self.E = E
        self.c = {n: 0 for n in self.sems}
        self.waited = {n: 0 for n in self.sems}
        self.ai = 0
        self.first_pass = eng == ENGS[0]
        if self.first_pass:
            self.allocs = []

    def on(self, *names):
        return self.eng in names

    def w(self, name, val):
        if val is None or val <= 0:
            return
        if val > self.waited[name]:
            self.E.wait_ge(self.sems[name], val)
            self.waited[name] = val

    def dep(self, deps):
        for d in deps:
            if d is not None and d[1] is not None:
                self.w(d[0], d[1])

    def op(self, eng, deps, f, mark=None, n=1):
        ins = None
        if self.on(eng):
            self.dep(deps)
            ins = f()
        if mark is not None:
            self.c[mark] += n
            if ins is not None:
                ins.then_inc(self.sems[mark], n)
            return self.c[mark]
        return None

    def barrier(self, skip=()):
        for nm in self.semnames:
            if nm not in skip:
                self.w(nm, self.c[nm])


SEMNAMES = (
    ["pe", "act", "dve", "pool", "dm", "dmg", "gw", "gw2", "gx", "cc"]
    + ["dx0", "dx1", "ws0", "ws1", "wf0", "wf1"]
    + ["sq0", "sq1", "sk0", "sk1", "sv0", "sv1", "sxn", "sc0", "sc1"]
    + ["dk0", "dk1", "dv0", "dv1", "du0", "du1"]
    + ["dxn", "dw0", "dw1", "dn0", "dn1", "do0", "do1", "dc0"]
    + ["so0", "so1"]
)




_EM = None


class _CachedGuard:

    def __init__(self, factory):
        self.factory = factory
        self.g = None

    def __enter__(self):
        em = _EM
        i = em.ai
        em.ai += 1
        if em.first_pass:
            self.g = self.factory()
            h = self.g.__enter__()
            em.allocs.append(h)
            return h
        return em.allocs[i]

    def __exit__(self, *a):
        if self.g is not None:
            return self.g.__exit__(*a)
        return False


def _sb(nc, name, shape, dt):
    return _CachedGuard(lambda: nc.sbuf_tensor(name, shape, dt, allow_name_mangling=True))


def _ps(nc, name, shape, dt=F32):
    return _CachedGuard(lambda: nc.psum_tensor(name, shape, dt, allow_name_mangling=True))


def p0(nc, em, C, t, pr, mk, ident16_h, ident32_h, onesc_h, onesr_h, epsc_h):
    D, F = C.D, C.F
    em.op("sync", [], lambda: nc.sync.dma_start(out=pr["epsc"][:, :], in_=epsc_h.ap()), "dm", 16)
    mk["dm_eps"] = em.c["dm"]
    em.op("sync", [], lambda: nc.sync.dma_start(out=pr["ident16"][:, :], in_=ident16_h.ap()), "dm", 16)
    em.op("sync", [], lambda: nc.sync.dma_start(out=pr["ident32"][:, :], in_=ident32_h.ap()), "dm", 16)
    em.op("sync", [], lambda: nc.sync.dma_start(out=pr["onesc"][:, :], in_=onesc_h.ap()), "dm", 16)
    em.op("sync", [], lambda: nc.sync.dma_start(out=pr["onesr"][:, :], in_=onesr_h.ap()), "dm", 16)

    st = min(512, D)
    for r0 in range(0, D, st):
        em.op("gpsimd", [], lambda r0=r0, st=st: nc.gpsimd.dma_start(
            out=t["qkvb"][r0:r0 + st, :], in_=t["qkv_w"][r0:r0 + st, :]), "gw", 16)
    mk["gw_qkv"] = em.c["gw"]
    gw2 = []
    st = min(1024, F)
    for r0 in range(0, F, st):
        gw2.append(lambda r0=r0, st=st: nc.gpsimd.dma_start(
            out=t["w12b"][r0:r0 + st, :], in_=t["w12"][r0:r0 + st, :]))
    st = min(1024, D)
    for r0 in range(0, D, st):
        gw2.append(lambda r0=r0, st=st: nc.gpsimd.dma_start(
            out=t["w3b"][r0:r0 + st, :], in_=t["w3"][r0:r0 + st, :]))
    for r0 in range(0, D, st):
        gw2.append(lambda r0=r0, st=st: nc.gpsimd.dma_start(
            out=t["owb"][r0:r0 + st, :], in_=t["out_w"][r0:r0 + st, :]))
    mk["gw2list"] = gw2
    mk["gw2_total"] = 16 * len(gw2)
    mk["gxlist"] = [lambda: nc.gpsimd.dma_start(
        out=t["out"][:, :], in_=t["x"][0:C.OWN, :])]

    with _sb(nc, "brows", [128, 128], F32) as brows, \
         _ps(nc, "bps", [128, 128], F32) as bps:
        em.op("sync", [], lambda: nc.sync.dma_start(
            out=brows[0:3 * C.H, :], in_=t["qkv_b"].rearrange("(b p) -> b p", p=128)), "dm", 16)
        v = em.op("tensor", [("dm", em.c["dm"])],
                  lambda: nc.tensor.transpose(bps[:, 0:3 * C.H], brows[0:3 * C.H, :],
                                              pr["ident32"][0:3 * C.H, 0:3 * C.H]), "pe")
        v2 = em.op("vector", [("pe", v)],
                   lambda: nc.vector.tensor_copy(pr["qbcol"][:, :], bps[:, 0:3 * C.H]), "dve")
        v2 = em.op("vector", [("dve", v2)],
                   lambda: nc.vector.tensor_scalar_mul(
                       pr["qbcolS"][:, :], pr["qbcol"][:, 0:C.H], 1.0 / math.sqrt(128.0)), "dve")
        em.op("sync", [("dve", v2)], lambda: nc.sync.dma_start(
            out=brows[0:C.FC, :], in_=t["b12"].rearrange("(b p) -> b p", p=128)), "dm", 16)
        v = em.op("tensor", [("dm", em.c["dm"]), ("dve", v2)],
                  lambda: nc.tensor.transpose(bps[:, 0:C.FC], brows[0:C.FC, :],
                                              pr["ident32"][0:C.FC, 0:C.FC]), "pe")
        em.op("vector", [("pe", v)],
              lambda: nc.vector.tensor_copy(pr["b12col"][:, :], bps[:, 0:C.FC]), "dve")


def p1(nc, em, C, t, pr, mk):
    D, H = C.D, C.H
    qkvb_r = t["qkvb"].rearrange("(kc p) n -> p kc n", p=128)

    ngr = 1 if C.kvx else 2
    for g in range(ngr):
        R0 = g * C.GRP
        with contextlib.ExitStack() as gs:
            xnfm = gs.enter_context(_sb(nc, "xnfm", [128, C.DC * C.GRP], BF16))
            cosg = gs.enter_context(_sb(nc, "cosg", [64, C.GRP], BF16))
            sing = gs.enter_context(_sb(nc, "sing", [64, C.GRP], BF16))

            em.barrier()
            em.op("gpsimd", [], lambda: nc.gpsimd.dma_start(
                out=cosg[:, :], in_=t["cosk"][:, R0:R0 + C.GRP]), "dmg", 16)
            em.op("gpsimd", [], lambda: nc.gpsimd.dma_start(
                out=sing[:, :], in_=t["sink"][:, R0:R0 + C.GRP]), "dmg", 16)
            vmisc = em.c["dmg"]


            with contextlib.ExitStack() as asb:
                xr = asb.enter_context(_sb(nc, "xr", [128, 2 * D], F32))
                xn16 = asb.enter_context(_sb(nc, "xn16", [128, 2 * D], BF16))
                xsq = asb.enter_context(_sb(nc, "xsq", [128, D], BF16))
                stats = asb.enter_context(_sb(nc, "stats", [128, 8], F32))
                tp0 = asb.enter_context(_ps(nc, "tp0", [128, 1024], BF16))
                tp1 = asb.enter_context(_ps(nc, "tp1", [128, 1024], BF16))
                tp = [tp0, tp1]
                for T in range(C.NB):
                    sl = T % 2
                    gate = [("dve", mk.get(("xnstt", g, T - 2))),
                            ("act", mk.get(("xsq", g, T - 2)))]
                    mk[("xload", g, T)] = em.op(
                        "sync", gate, lambda T=T, sl=sl: nc.sync.dma_start(
                            out=xr[:, sl * D:(sl + 1) * D],
                            in_=t["x"][R0 + T * 128:R0 + T * 128 + 128, :]),
                        f"dx{sl}", 16)
                    dxv = [(f"dx{sl}", mk[("xload", g, T)])]
                    vsum = em.op("vector", dxv + [("act", mk.get(("negmu", g, T - 1)))],
                                 lambda sl=sl: nc.vector.reduce_sum(
                                     stats[:, 0:1], xr[:, sl * D:(sl + 1) * D], axis=AX.X), "dve")
                    mk[("xsq", g, T)] = em.op(
                        "scalar", dxv + [("act", mk.get(("xsq", g, T - 1))),
                                         ("dve", mk.get(("varr", g, T - 1)))],
                        lambda sl=sl: nc.scalar.activation(
                            xsq[:, :], xr[:, sl * D:(sl + 1) * D],
                            AF.Square, accum_out=stats[:, 1:2]), "act")
                    mk[("negmu", g, T)] = em.op(
                        "scalar", [("dve", vsum), ("dve", mk.get(("xnstt", g, T - 1)))],
                        lambda: nc.scalar.mul(stats[:, 2:3], stats[:, 0:1], -1.0 / D), "act")
                    vmu2 = em.op("scalar", [("act", mk[("negmu", g, T)]),
                                            ("dve", mk.get(("varr", g, T - 1)))],
                                 lambda: nc.scalar.activation(stats[:, 3:4], stats[:, 2:3],
                                                              AF.Square), "act")
                    mk[("varr", g, T)] = em.op(
                        "vector", [("act", vmu2), ("act", mk[("xsq", g, T)]),
                                   ("act", mk.get(("rstdo", g, T - 1)))],
                        lambda: nc.vector.scalar_tensor_tensor(
                            stats[:, 4:5], stats[:, 1:2], 1.0 / D, stats[:, 3:4],
                            ALU.mult, ALU.subtract), "dve")
                    vln = em.op("scalar", [("dve", mk[("varr", g, T)]), ("dm", mk["dm_eps"])],
                                lambda: nc.scalar.activation(stats[:, 5:6], stats[:, 4:5],
                                                             AF.Ln, bias=pr["epsc"][:, 0:1]), "act")
                    mk[("rstdo", g, T)] = em.op(
                        "scalar", [("act", vln), ("dve", mk.get(("xnstt", g, T - 1)))],
                        lambda: nc.scalar.activation(stats[:, 6:7], stats[:, 5:6],
                                                     AF.Exp, scale=-0.5), "act")
                    mk[("xnstt", g, T)] = em.op(
                        "vector", [("act", mk[("rstdo", g, T)]),
                                   ("pe", mk.get(("tpdone", g, T - 2)))],
                        lambda sl=sl: nc.vector.tensor_scalar(
                            xn16[:, sl * D:(sl + 1) * D], xr[:, sl * D:(sl + 1) * D],
                            stats[:, 2:3], stats[:, 6:7], ALU.add, ALU.mult), "dve")
                    GB = min(8, C.DC)
                    for kc in range(C.DC):
                        idx = T * C.DC + kc
                        bank = (idx // GB) % 2
                        pos = idx % GB
                        deps = [("dve", mk[("xnstt", g, T)]), ("dm", 16)]
                        if pos == 0:
                            deps.append(("dve", mk.get(("tpev", g, (idx // GB) - 2))))
                        last = (pos == GB - 1) or (idx == C.NB * C.DC - 1)
                        v = em.op("tensor", deps,
                                  lambda kc=kc, sl=sl, bank=bank, pos=pos: nc.tensor.transpose(
                                      tp[bank][:, pos * 128:(pos + 1) * 128],
                                      xn16[:, sl * D + kc * 128: sl * D + (kc + 1) * 128],
                                      pr["ident16"][:, :]),
                                  "pe" if last else None)
                        if last:
                            mk[("tpdone", g, T)] = v
                            b = idx // GB
                            npos = pos + 1
                            kc0 = (b * GB) % C.DC
                            T0 = (b * GB) // C.DC

                            def evac(b=b, bank=bank, npos=npos, kc0=kc0, T0=T0):
                                dst = xnfm[:, :].rearrange("p (kc u) -> p kc u", u=C.GRP)
                                dst = dst[:, kc0:kc0 + npos, T0 * 128:T0 * 128 + 128]
                                src = tp[bank][:, 0:npos * 128].rearrange(
                                    "p (kc u) -> p kc u", u=128)
                                return nc.vector.tensor_copy(dst, src)
                            mk[("tpev", g, b)] = em.op("vector", [("pe", v)], evac, "dve")
                mk[("xnfm", g)] = mk[("tpev", g, (C.NB * C.DC - 1) // min(8, C.DC))]

            if g == 0:
                for kc in range(C.DC):
                    em.op("scalar", [("dve", mk[("xnfm", 0)])],
                          lambda kc=kc: nc.scalar.dma_start(
                              out=t["xnst"][kc], in_=xnfm[:, kc * C.GRP:(kc + 1) * C.GRP]),
                          "sxn", 16)
                mk["sxn"] = em.c["sxn"]

            em.barrier()
            with contextlib.ExitStack() as bsb:
                wsl = [bsb.enter_context(_sb(nc, f"wsl{j}", [128, C.DC * 256], BF16))
                       for j in range(2)]
                wf32 = [bsb.enter_context(_sb(nc, f"wf32{j}",
                        [128, max(1, C.DC // max(4, C.DC // 4)) * 256], F32)) for j in range(2)]
                riL = [bsb.enter_context(_sb(nc, f"riL{j}", [64, C.FT], F32))
                       for j in range(2)]
                riH = [bsb.enter_context(_sb(nc, f"riH{j}", [64, C.FT], F32))
                       for j in range(2)]
                ttb = bsb.enter_context(_sb(nc, "ttb", [64, 4 * C.FT], F32))
                o16L = [bsb.enter_context(_sb(nc, f"o16L{j}", [64, C.FT], BF16))
                        for j in range(2)]
                o16H = [bsb.enter_context(_sb(nc, f"o16H{j}", [64, C.FT], BF16))
                        for j in range(2)]
                o16v = [bsb.enter_context(_sb(nc, f"o16v{j}", [128, C.FT], BF16))
                        for j in range(2)]
                qkps = [bsb.enter_context(_ps(nc, f"qk{j}", [128, 512], F32))
                        for j in range(8)]

                blocks = []
                blocks += [("k", h) for h in range(H)]
                blocks += [("v", h) for h in range(H)]
                if g == 0:
                    blocks += [("q", h) for h in range(H)]
                nbat = len(blocks) // 2
                rnd = 0
                scnt = 0
                for bi in range(nbat):
                    kind, h0 = blocks[2 * bi]
                    base = {"q": 0, "k": D, "v": 2 * D}[kind]
                    col0 = base + h0 * 128
                    ssl = bi % 2
                    KG = min(4, C.DC)
                    NCAST = 0
                    if bi < NCAST:
                        gate = [("pe", mk.get(("slabfree", g, bi - 2)))]
                        qw_r = t["qkv_w"].rearrange("(kc p) n -> p kc n", p=128)
                        nhf = max(4, C.DC // 4)
                        hw = C.DC // nhf
                        for hf in range(nhf):
                            rr = hf % 2
                            dep2 = [("dve", mk.get(("bcast", g, bi, hf - 2)) if hf >= 2
                                     else mk.get(("bcast", g, bi - 1, nhf - 2 + hf)))]
                            em.op("sync", (gate if hf == 0 else []) + dep2,
                                  lambda col0=col0, hf=hf, hw=hw, rr=rr: nc.sync.dma_start(
                                      out=wf32[rr][:, :].rearrange("p (k n) -> p k n", n=256),
                                      in_=qw_r[:, hf * hw:(hf + 1) * hw, col0:col0 + 256]),
                                  f"wf{rr}", 16)
                            mk[("bcast", g, bi, hf)] = em.op(
                                "vector", [(f"wf{rr}", em.c[f"wf{rr}"])],
                                lambda ssl=ssl, hf=hf, hw=hw, rr=rr: nc.vector.tensor_copy(
                                    wsl[ssl][:, hf * hw * 256:(hf + 1) * hw * 256],
                                    wf32[rr][:, :]), "dve")
                        mk[("wsl", g, bi)] = 0
                        mk[("wslc", g, bi)] = ("dve", mk[("bcast", g, bi, nhf - 1)])
                    else:
                        gate = [("pe", mk.get(("slabfree", g, bi - 2))),
                                ("gw", mk["gw_qkv"])]
                        for kg in range(C.DC // KG):
                            em.op("sync", gate if kg == 0 else [],
                                  lambda col0=col0, kg=kg, ssl=ssl, KG=KG: nc.sync.dma_start(
                                      out=wsl[ssl][:, kg * KG * 256:(kg + 1) * KG * 256]
                                      .rearrange("p (k n) -> p k n", n=256),
                                      in_=qkvb_r[:, kg * KG:(kg + 1) * KG, col0:col0 + 256]),
                                  f"ws{ssl}", 16)
                        mk[("wsl", g, bi)] = em.c[f"ws{ssl}"]
                        mk[("wslc", g, bi)] = None
                    for dl in range(2):
                        for kc in range(C.DC):
                            for tau in range(C.TAU):
                                bank = dl * C.TAU + tau
                                deps = [(f"ws{ssl}", mk[("wsl", g, bi)]),
                                        mk.get(("wslc", g, bi)),
                                        ("dve", mk[("xnfm", g)])]
                                if kc == 0:
                                    deps.append(("act", mk.get(("riev", g, bi - 1, dl, tau))))
                                markit = (kc == C.DC - 1) or (
                                    dl == 1 and kc == C.DC - 1 and tau == C.TAU - 1)
                                v = em.op("tensor", deps,
                                          lambda bank=bank, kc=kc, tau=tau, ssl=ssl, dl=dl:
                                          nc.tensor.matmul(
                                              qkps[bank][:, 0:C.FT],
                                              wsl[ssl][:, kc * 256 + dl * 128:
                                                       kc * 256 + (dl + 1) * 128],
                                              xnfm[:, kc * C.GRP + tau * C.FT:
                                                   kc * C.GRP + (tau + 1) * C.FT],
                                              start=(kc == 0), stop=(kc == C.DC - 1)),
                                          "pe" if markit else None)
                                if kc == C.DC - 1:
                                    mk[("fill", g, bi, dl, tau)] = v
                        if dl == 1:
                            mk[("slabfree", g, bi)] = v
                    for dl in range(2):
                        h = h0 + dl
                        bcol = {"q": 0, "k": H, "v": 2 * H}[kind] + h
                        for tau in range(C.TAU):
                            bank = dl * C.TAU + tau
                            r = rnd % 2
                            ccol = (tau * C.FT, (tau + 1) * C.FT)
                            if kind == "v":
                                deps = [("pe", mk[("fill", g, bi, dl, tau)]),
                                        mk.get(("o16st", rnd - 2))]
                                va = em.op("scalar", deps,
                                           lambda bank=bank, r=r, bcol=bcol: nc.scalar.activation(
                                               o16v[r][:, :], qkps[bank][:, 0:C.FT], AF.Identity,
                                               bias=pr["qbcol"][:, bcol:bcol + 1]), "act")
                                mk[("riev", g, bi, dl, tau)] = va
                                dstap = (t["vin"][h] if C.kvx else t["vst"][h // 4, g, h % 4])[:, ccol[0]:ccol[1]]
                                so = f"sv{rnd % 2}"
                                sv = em.op("scalar", [("act", va)],
                                           lambda dstap=dstap, r=r: nc.scalar.dma_start(
                                               out=dstap, in_=o16v[r][:, :]), so, 16)
                                mk[("o16st", rnd)] = (so, sv)
                                mk.setdefault(("hst", kind, h), []).append((so, sv))
                                rnd += 1
                                HCG = min(4, H)
                                if C.kvx and tau == C.TAU - 1 and h % HCG == HCG - 1:
                                    grp = [[2 * b, 2 * b + 1] for b in range(C.ncc // 2)]
                                    h0g = h - HCG + 1
                                    deps = []
                                    for hh in range(h0g, h + 1):
                                        deps += mk[("hst", "v", hh)]
                                    em.op("gpsimd", deps,
                                          lambda h0g=h0g, HCG=HCG, grp=grp:
                                          nc.gpsimd.collective_compute(
                                              "AllGather", ALU.bypass,
                                              replica_groups=grp,
                                              ins=[t["vin"][h0g:h0g + HCG].rearrange(
                                                  "h p u -> (h p) u").opt()],
                                              outs=[t["vst"][h0g // HCG].rearrange(
                                                  "g h p u -> (g h p) u").opt()]),
                                          "cc", 1)
                                    for hh in range(h0g, h + 1):
                                        mk[("cch", hh)] = em.c["cc"]
                                continue
                            deps = [("pe", mk[("fill", g, bi, dl, tau)]),
                                    ("dve", mk.get(("rope", rnd - 2))),
                                    ("pool", mk.get(("prope", rnd - 2)))]
                            if kind == "q":
                                bap, bsc = pr["qbcolS"], C.scale
                                bq = h
                            else:
                                bap, bsc = pr["qbcol"], 1.0
                                bq = bcol
                            va = em.op("scalar", deps,
                                       lambda bank=bank, r=r, bq=bq, bap=bap, bsc=bsc:
                                       nc.scalar.activation(
                                           riL[r][:, :], qkps[bank][0:64, 0:C.FT], AF.Identity,
                                           bias=bap[0:64, bq:bq + 1], scale=bsc), "act")
                            va = em.op("scalar", [],
                                       lambda bank=bank, r=r, bq=bq, bap=bap, bsc=bsc:
                                       nc.scalar.activation(
                                           riH[r][:, :], qkps[bank][64:128, 0:C.FT], AF.Identity,
                                           bias=bap[64:128, bq:bq + 1], scale=bsc), "act")
                            mk[("riev", g, bi, dl, tau)] = va
                            sc = C.scale if kind == "q" else 1.0
                            arid = ("act", va)
                            cosu = cosg
                            sinu = sing
                            tdep = ("dmg", vmisc)
                            e1 = em.op("vector",
                                       [arid, tdep, ("dve", mk.get(("rope", rnd - 1)))],
                                       lambda r=r, cosu=cosu, ccol=ccol: nc.vector.tensor_mul(
                                           ttb[:, 0:C.FT], riL[r][:, :],
                                           cosu[:, ccol[0]:ccol[1]]), "dve")
                            e2 = em.op("vector", [arid, ("dve", e1)],
                                       lambda r=r, sinu=sinu, ccol=ccol: nc.vector.tensor_mul(
                                           ttb[:, C.FT:2 * C.FT], riH[r][:, :],
                                           sinu[:, ccol[0]:ccol[1]]), "dve")
                            e3 = em.op("vector", [("dve", e2), mk.get(("o16st", rnd - 2))],
                                       lambda r=r: nc.vector.tensor_sub(
                                           o16L[r][:, :], ttb[:, 0:C.FT],
                                           ttb[:, C.FT:2 * C.FT]), "dve")
                            mk[("rope", rnd)] = e3
                            p1_ = em.op("gpsimd",
                                        [arid, tdep, ("pool", mk.get(("prope", rnd - 1)))],
                                        lambda r=r, cosu=cosu, ccol=ccol: nc.gpsimd.tensor_mul(
                                            ttb[:, 2 * C.FT:3 * C.FT], riH[r][:, :],
                                            cosu[:, ccol[0]:ccol[1]]), "pool")
                            p2_ = em.op("gpsimd", [arid, ("pool", p1_)],
                                        lambda r=r, sinu=sinu, ccol=ccol: nc.gpsimd.tensor_mul(
                                            ttb[:, 3 * C.FT:4 * C.FT], riL[r][:, :],
                                            sinu[:, ccol[0]:ccol[1]]), "pool")
                            p3_ = em.op("gpsimd", [("pool", p2_), mk.get(("o16st", rnd - 2))],
                                        lambda r=r: nc.gpsimd.tensor_add(
                                            o16H[r][:, :], ttb[:, 2 * C.FT:3 * C.FT],
                                            ttb[:, 3 * C.FT:4 * C.FT]), "pool")
                            mk[("prope", rnd)] = p3_
                            if kind == "q":
                                dstap = t["qst"][h]
                                so = f"sq{rnd % 2}"
                            else:
                                dstap = t["kin"][h] if C.kvx else t["kst"][h // 4, g, h % 4]
                                so = f"sk{rnd % 2}"
                            em.op("scalar", [("dve", e3)],
                                  lambda dstap=dstap, r=r, ccol=ccol: nc.scalar.dma_start(
                                      out=dstap[0:64, ccol[0]:ccol[1]], in_=o16L[r][:, :]), so, 16)
                            sv = em.op("scalar", [("pool", p3_)],
                                       lambda dstap=dstap, r=r, ccol=ccol: nc.scalar.dma_start(
                                           out=dstap[64:128, ccol[0]:ccol[1]], in_=o16H[r][:, :]),
                                       so, 16)
                            mk[("o16st", rnd)] = (so, sv)
                            if kind == "k":
                                mk.setdefault(("hst", "k", h), []).append((so, sv))
                            rnd += 1
                            HCG = min(4, H)
                            if C.kvx and kind == "k" and tau == C.TAU - 1 \
                                    and h % HCG == HCG - 1:
                                grp = [[2 * b, 2 * b + 1] for b in range(C.ncc // 2)]
                                h0g = h - HCG + 1
                                deps = []
                                for hh in range(h0g, h + 1):
                                    deps += mk[("hst", "k", hh)]
                                em.op("gpsimd", deps,
                                      lambda h0g=h0g, HCG=HCG, grp=grp:
                                      nc.gpsimd.collective_compute(
                                          "AllGather", ALU.bypass,
                                          replica_groups=grp,
                                          ins=[t["kin"][h0g:h0g + HCG].rearrange(
                                              "h p u -> (h p) u").opt()],
                                          outs=[t["kst"][h0g // HCG].rearrange(
                                              "g h p u -> (g h p) u").opt()]),
                                      "cc", 1)


def p2(nc, em, C, t, pr, mk):
    H, S, TT = C.H, C.S, C.TT
    while mk.get("gw2list"):
        em.op("gpsimd", [], mk["gw2list"].pop(0), "gw2", 16)
    while mk.get("gxlist"):
        em.op("gpsimd", [], mk["gxlist"].pop(0), "gx", 16)
    em.barrier(skip=("cc", "gw2", "gw", "gx", "sxn"))
    with contextlib.ExitStack() as es:
        kh = [es.enter_context(_sb(nc, f"kh{j}", [128, S], BF16)) for j in range(2)]
        vh = [es.enter_context(_sb(nc, f"vh{j}", [128, S], BF16)) for j in range(2)]
        qh = [es.enter_context(_sb(nc, f"qh{j}", [128, C.OWN], BF16)) for j in range(2)]
        vtm = [es.enter_context(_sb(nc, f"vtm{j}", [128, S], BF16)) for j in range(2)]
        mA = es.enter_context(_sb(nc, "mA", [128, C.CPT * TT], BF16))
        mB = es.enter_context(_sb(nc, "mB", [128, C.CPT * TT], BF16))
        ex = [es.enter_context(_sb(nc, f"ex{j}", [128, TT], BF16)) for j in range(3)]
        rec = es.enter_context(_sb(nc, "rec", [1, 4 * TT], F32))
        rb = es.enter_context(_sb(nc, "rb", [128, TT], F32))
        cx = [es.enter_context(_sb(nc, f"cx{j}", [128, C.OWN], BF16)) for j in range(2)]
        ctxp = [es.enter_context(_ps(nc, f"ctx{s}", [128, 512], F32)) for s in range(4)]
        scp = [es.enter_context(_ps(nc, f"scp{j}", [128, 512], F32)) for j in range(2)]
        dnp = es.enter_context(_ps(nc, "dnp", [128, 512], F32))
        vtp = es.enter_context(_ps(nc, "vtp", [128, 1024], BF16))

        em.op("sync", [], lambda: nc.sync.dma_start(
            out=mA[:, :].rearrange("p (d u) -> p d u", u=TT),
            in_=t["maskA"].rearrange("d p u -> p d u")), "dm", 16)
        em.op("sync", [], lambda: nc.sync.dma_start(
            out=mB[:, :].rearrange("p (d u) -> p d u", u=TT),
            in_=t["maskB"].rearrange("d p u -> p d u")), "dm", 16)
        vmask = em.c["dm"]

        seq = []
        for sec in range(2):
            for c in range(C.SEC):
                for s in range(c // C.CPT, 4):
                    seq.append((sec, c, s))
        nidx = len(seq)
        first = {}
        last = {}
        for i, (sec, c, s) in enumerate(seq):
            first.setdefault(s, i)
            last[s] = i
        NR8 = C.NCH // 8

        def loads(h):
            sl = h % 2
            gate = [("pe", mk.get(("pv", h - 2, nidx - 1)))]
            if C.kvx:
                gate = gate + [("cc", mk.get(("cch", h), em.c["cc"]))]
            HCG = min(4, C.H)
            ksrc = t["kst"][h // HCG, :, h % HCG].rearrange("g p u -> p g u")
            mk[("khl", h)] = em.op("sync", gate, lambda ksrc=ksrc, sl=sl: nc.sync.dma_start(
                out=kh[sl][:, :].rearrange("p (g u) -> p g u", g=2), in_=ksrc), f"dk{sl}", 16)
            vgate = [("pe", mk.get(("vtpdone", h - 2, NR8 - 1)))]
            if C.kvx:
                vgate = vgate + [("cc", mk.get(("cch", h), em.c["cc"]))]
            vsrc = t["vst"][h // HCG, :, h % HCG].rearrange("g p u -> p g u")
            mk[("vhl", h)] = em.op("sync", vgate, lambda vsrc=vsrc, sl=sl: nc.sync.dma_start(
                out=vh[sl][:, :].rearrange("p (g u) -> p g u", g=2), in_=vsrc), f"dv{sl}", 16)
            mk[("qhl", h)] = em.op("sync", gate, lambda h=h, sl=sl: nc.sync.dma_start(
                out=qh[sl][:, :], in_=t["qst"][h]), f"du{sl}", 16)

        def vtrans_round(h, rb8):
            sl = h % 2
            for j in range(8):
                cidx = rb8 * 8 + j
                deps = [(f"dv{sl}", mk[("vhl", h)]), ("dm", 16)]
                if j == 0:
                    if rb8 == 0:
                        deps.append(("dve", mk.get(("vtev", h - 1, NR8 - 1))))
                        deps.append(("pe", mk.get(("pv", h - 2, nidx - 1))))
                    else:
                        deps.append(("dve", mk.get(("vtev", h, rb8 - 1))))
                v = em.op("tensor", deps,
                          lambda cidx=cidx, j=j, sl=sl: nc.tensor.transpose(
                              vtp[:, j * 128:(j + 1) * 128],
                              vh[sl][:, cidx * 128:(cidx + 1) * 128],
                              pr["ident16"][:, :]),
                          "pe" if j == 7 else None)
            mk[("vtpdone", h, rb8)] = v
            mk[("vtev", h, rb8)] = em.op(
                "vector", [("pe", v)], lambda rb8=rb8, sl=sl: nc.vector.tensor_copy(
                    vtm[sl][:, rb8 * 1024:(rb8 + 1) * 1024], vtp[:, :]), "dve")

        def pvpair(h, i, sl):
            sec, c, s = seq[i]
            kcol = sec * C.OWN + c * 128
            st = (i == first[s])
            sp = (i == last[s])
            pvdeps = [("act", mk[("exp", h, i)]), ("dve", mk.get(("vtev", h, NR8 - 1)))]
            if st:
                pvdeps.append(("dve", mk.get(("cxm", h - 1, s))))
            em.op("tensor", pvdeps, lambda s=s, kcol=kcol, i=i, sl=sl: nc.tensor.matmul(
                ctxp[s][:, 0:TT], vtm[sl][:, kcol:kcol + 128], ex[(h * nidx + i) % 3][:, :],
                start=st, stop=sp), None)
            dndeps = []
            if st:
                dndeps.append(("dve", mk.get(("rbcp", h - 1, 3))))
            mk[("pv", h, i)] = em.op(
                "tensor", dndeps, lambda s=s, i=i, st=st, sp=sp: nc.tensor.matmul(
                    dnp[32 * s:32 * s + 1, 0:TT], pr["onesc"][:, :], ex[(h * nidx + i) % 3][:, :],
                    start=st, stop=sp,
                    tile_position=(0, 32 * s)), "pe")

        trigger = {16 + r * 16: r for r in range(NR8)} if nidx >= 16 * NR8 + 16 else             {4 + r * 4: r for r in range(NR8)}
        for h in range(H):
            sl = h % 2
            if h == 0:
                loads(0)
                for r in range(NR8):
                    vtrans_round(0, r)
            if h + 1 < H:
                loads(h + 1)
            for i in range(nidx):
                sec, c, s = seq[i]
                gi = h * nidx + i
                sb = scp[gi % 2]
                kcol = sec * C.OWN + c * 128
                diag = (c // C.CPT == s)
                d = c % C.CPT
                deps = [(f"dk{sl}", mk[("khl", h)]), (f"du{sl}", mk[("qhl", h)]),
                        ("dm", vmask),
                        ("act", mk.get(("exp", h, i - 2), mk.get(("exp", h - 1, nidx - 1))))]
                v = em.op("tensor", deps, lambda sb=sb, kcol=kcol, s=s, diag=diag, sl=sl:
                          nc.tensor.matmul(
                              sb[:, 0:TT], kh[sl][:, kcol:kcol + 128],
                              qh[sl][:, s * TT:(s + 1) * TT],
                              start=True, stop=not diag), "pe" if not diag else None)
                if diag:
                    mm = mA if sec == 0 else mB
                    v = em.op("tensor", [], lambda sb=sb, mm=mm, d=d: nc.tensor.matmul(
                        sb[:, 0:TT], pr["ident16"][:, :], mm[:, d * TT:(d + 1) * TT],
                        start=False, stop=True), "pe")
                mk[("score", h, i)] = v
                pgi = gi - 3
                ph, pi = divmod(pgi, nidx) if pgi >= 0 else (None, None)
                deps = [("pe", v)]
                if ph is not None:
                    deps.append(("pe", mk.get(("pv", ph, pi))))
                mk[("exp", h, i)] = em.op("scalar", deps, lambda sb=sb, gi=gi:
                                          nc.scalar.activation(
                                              ex[gi % 3][:, :], sb[:, 0:TT], AF.Exp), "act")
                if i >= 1:
                    pvpair(h, i - 1, sl)
                if h + 1 < H and i in trigger:
                    vtrans_round(h + 1, trigger[i])
            pvpair(h, nidx - 1, sl)

            for s in range(4):
                mk[("recip", h, s)] = em.op(
                    "vector", [("pe", mk[("pv", h, last[s])]),
                               ("pe", mk.get(("bcmm", h - 1, s)))],
                    lambda s=s: nc.vector.reciprocal(
                        rec[0:1, s * TT:(s + 1) * TT],
                        dnp[32 * s:32 * s + 1, 0:TT]), "dve")
            for s in range(4):
                vb = em.op("tensor", [("dve", mk[("recip", h, 3)]),
                                      ("dve", mk.get(("rbcp", h, s - 1)))],
                           lambda s=s: nc.tensor.matmul(
                               dnp[:, 0:TT], pr["onesr"][:, :],
                               rec[0:1, s * TT:(s + 1) * TT],
                               start=True, stop=True), "pe")
                mk[("bcmm", h, s)] = vb
                vc = em.op("vector", [("pe", vb), ("dve", mk.get(("cxm", h, s - 1)))],
                           lambda: nc.vector.tensor_copy(rb[:, :], dnp[:, 0:TT]), "dve")
                mk[("rbcp", h, s)] = vc
                scs = f"sc{h % 2}"
                mk[("cxm", h, s)] = em.op(
                    "vector", [("dve", vc), (scs, mk.get((scs + "cnt", h - 2), 0))],
                    lambda h=h, s=s: nc.vector.tensor_mul(
                        cx[h % 2][:, s * TT:(s + 1) * TT], ctxp[s][:, 0:TT], rb[:, :]), "dve")
            mk[(f"sc{h % 2}cnt", h)] = em.op(
                "scalar", [("dve", mk[("cxm", h, 3)])],
                lambda h=h: nc.scalar.dma_start(out=t["cxst"][h], in_=cx[h % 2][:, :]),
                f"sc{h % 2}", 16)


def p4(nc, em, C, t, pr, mk):
    D, H, M = C.D, C.H, C.M
    em.barrier()
    with contextlib.ExitStack() as es:
        bsb = es.enter_context(_sb(nc, "bsb", [128, D], BF16))
        x4 = es.enter_context(_sb(nc, "x4", [128, C.MC * C.GP4], BF16))

        with _sb(nc, "obr", [1, D], F32) as obr, \
             _sb(nc, "b3r", [1, D], F32) as b3r, \
             _ps(nc, "bps2", [128, 512], F32) as bps2:
            em.op("sync", [], lambda: nc.sync.dma_start(
                out=obr[:, :], in_=t["out_b"].unsqueeze(0)), "dm", 16)
            em.op("sync", [], lambda: nc.sync.dma_start(
                out=b3r[:, :], in_=t["b3"].unsqueeze(0)), "dm", 16)
            va = em.op("vector", [("dm", em.c["dm"])],
                       lambda: nc.vector.tensor_add(obr[:, :], obr[:, :], b3r[:, :]), "dve")
            vprev = None
            for j in range(C.DT):
                v = em.op("tensor", [("dve", va), ("dve", vprev)],
                          lambda j=j: nc.tensor.matmul(bps2[:, 0:512], pr["onesr"][:, :],
                                                       obr[:, j * 512:(j + 1) * 512],
                                                       start=True, stop=True), "pe")
                vprev = em.op("vector", [("pe", v)],
                              lambda j=j: nc.vector.tensor_copy(
                                  bsb[:, j * 512:(j + 1) * 512], bps2[:, 0:512]), "dve")

        for g4 in range(C.NG4):
            Q0 = g4 * C.GP4
            nblk = min(4, C.MC)
            ni4 = C.MC // nblk
            with contextlib.ExitStack() as ws:
                xnp4 = ws.enter_context(_sb(nc, "xnp4", [128, C.DC * C.GP4], BF16))
                w1s = [ws.enter_context(_sb(nc, "w1s0", [128, C.DC * 512], BF16))]
                w2s = [ws.enter_context(_sb(nc, "w2s0", [128, C.DC * 512], BF16))]
                s1b = ws.enter_context(_sb(nc, "s1b", [128, C.GP4], F32))
                fps = [ws.enter_context(_ps(nc, f"fp{j}", [128, 512], F32))
                       for j in range(8)]
                em.barrier()
                for kc in range(C.DC):
                    em.op("sync", [("sxn", mk.get("sxn", 0))],
                          lambda kc=kc, Q0=Q0: nc.sync.dma_start(
                              out=xnp4[:, kc * C.GP4:(kc + 1) * C.GP4],
                              in_=t["xnst"][kc][:, Q0:Q0 + C.GP4]), "dxn", 16)
                mk[("xnp4", g4)] = em.c["dxn"]
                for i4 in range(ni4):
                    ssl = 0
                    GK = min(8, C.DC)
                    for kg in range(C.DC // GK):
                        p_ = kg % 2
                        kcl = kg * GK + GK - 1
                        gate = [("pe", mk.get(("w12kc", g4, i4 - 1, kcl))),
                                (f"dw{p_}", mk.get(("dwq", g4, i4, kg - 2),
                                                   mk.get(("dwqp", g4, i4 - 1, p_))))]
                        if i4 == 0:
                            gate.append(("gw2", mk["gw2_total"]))
                        for kc in range(kg * GK, (kg + 1) * GK):
                            for half, wdst in ((0, w1s), (1, w2s)):
                                r0 = half * M + i4 * 512
                                em.op("sync", gate if (half == 0 and kc == kg * GK) else [],
                                      lambda wdst=wdst, kc=kc, r0=r0: nc.sync.dma_start(
                                          out=wdst[0][:, kc * 512:(kc + 1) * 512],
                                          in_=t["w12b"][r0:r0 + 512, kc * 128:(kc + 1) * 128],
                                          transpose=True),
                                      f"dw{p_}", 16)
                        mk[("dwq", g4, i4, kg)] = em.c[f"dw{p_}"]
                        mk[("dwqp", g4, i4, p_)] = em.c[f"dw{p_}"]
                        for kc in range(kg * GK, (kg + 1) * GK):
                            mk[("wslkc", g4, i4, kc)] = (f"dw{p_}", em.c[f"dw{p_}"])
                    for ih in range(nblk // 2):
                        ib0 = ih * 2
                        for kc in range(C.DC):
                            for ib2 in range(2):
                                ib = ib0 + ib2
                                i = i4 * nblk + ib
                                for half in range(2):
                                    wsl_ = w1s if half == 0 else w2s
                                    for tau in range(C.T4):
                                        bank = ib2 * 4 + half * 2 + tau
                                        deps = [mk[("wslkc", g4, i4, kc)],
                                                ("dxn", mk[("xnp4", g4)])]
                                        if kc == 0:
                                            deps += [("act", mk.get(("silu", g4, i - 2, tau))),
                                                     ("dve", mk.get(("x4ev", g4, i - 2, tau)))]
                                        markit = (kc == C.DC - 1) or (
                                            ih == nblk // 2 - 1 and ib2 == 1
                                            and half == 1 and tau == C.T4 - 1)
                                        v = em.op(
                                            "tensor", deps,
                                            lambda bank=bank, kc=kc, tau=tau, wsl_=wsl_, ib=ib:
                                            nc.tensor.matmul(
                                                fps[bank][:, 0:C.FT4],
                                                wsl_[0][:, kc * 512 + ib * 128:
                                                        kc * 512 + ib * 128 + 128],
                                                xnp4[:, kc * C.GP4 + tau * C.FT4:
                                                     kc * C.GP4 + (tau + 1) * C.FT4],
                                                start=(kc == 0), stop=(kc == C.DC - 1)),
                                            "pe" if markit else None)
                                        if ih == nblk // 2 - 1 and ib2 == 1 and half == 1 \
                                                and tau == C.T4 - 1:
                                            mk[("w12kc", g4, i4, kc)] = v
                                        if kc == C.DC - 1:
                                            mk[("x1fill" if half == 0 else "x2fill",
                                                g4, i4, ib, tau)] = v
                        for ib2 in range(2):
                            ib = ib0 + ib2
                            i = i4 * nblk + ib
                            for tau in range(C.T4):
                                b1 = ib2 * 4 + tau
                                b2 = ib2 * 4 + 2 + tau
                                vs = em.op("scalar", [("pe", mk[("x1fill", g4, i4, ib, tau)]),
                                                      ("dve", mk.get(("x4ev", g4, i - 1, tau)))],
                                           lambda b1=b1, tau=tau, i=i: nc.scalar.activation(
                                               s1b[:, tau * C.FT4:(tau + 1) * C.FT4],
                                               fps[b1][:, 0:C.FT4],
                                               AF.Silu, bias=pr["b12col"][:, i:i + 1]), "act")
                                mk[("silu", g4, i, tau)] = vs
                                deps = [("act", vs), ("pe", mk[("x2fill", g4, i4, ib, tau)]),
                                        ("pe", mk.get(("p4bdone", g4 - 1)))]
                                mk[("x4ev", g4, i, tau)] = em.op(
                                    "vector", deps,
                                    lambda b2=b2, i=i, tau=tau: nc.vector.scalar_tensor_tensor(
                                        x4[:, i * C.GP4 + tau * C.FT4:
                                           i * C.GP4 + (tau + 1) * C.FT4],
                                        fps[b2][:, 0:C.FT4],
                                        pr["b12col"][:, C.MC + i:C.MC + i + 1],
                                        s1b[:, tau * C.FT4:(tau + 1) * C.FT4],
                                        ALU.add, ALU.mult), "dve")
                mk[("w12don", g4)] = mk[("x4ev", g4, C.MC - 1, C.T4 - 1)]

            with contextlib.ExitStack() as bs:
                cxs = bs.enter_context(_sb(nc, "cxs", [128, C.DC * C.QSB], BF16))
                w3s = bs.enter_context(_sb(nc, "w3s", [128, C.MC * 512], BF16))
                ows = bs.enter_context(_sb(nc, "ows", [128, C.DC * 512], BF16))
                ot = [bs.enter_context(_sb(nc, f"ot{j}", [128, 512], F32))
                      for j in range(2)]
                ops_ = [bs.enter_context(_ps(nc, f"op{j}", [128, 512], F32))
                        for j in range(8)]
                em.barrier()
                ornd = 0
                hMC = max(1, C.MC // 2)
                hDC = max(1, C.DC // 2)
                for qs in range(C.NQS):
                    QQ = Q0 + qs * C.QSB
                    gate = [("pe", mk.get(("obank", g4, qs - 1, C.DT - 1, C.QB - 1)))]
                    if em.on("sync"):
                        em.w("sc0", em.c["sc0"])
                        em.w("sc1", em.c["sc1"])
                    for hh in range(H):
                        em.op("sync", gate if hh == 0 else [],
                              lambda hh=hh, QQ=QQ: nc.sync.dma_start(
                                  out=cxs[:, hh * C.QSB:(hh + 1) * C.QSB],
                                  in_=t["cxst"][hh][:, QQ:QQ + C.QSB]), "dc0", 16)
                    mk[("cxs", g4, qs)] = em.c["dc0"]
                    for j in range(C.DT):
                        GK3 = min(8, C.MC)
                        for kg in range(C.MC // GK3):
                            p_ = kg % 2
                            kcl = kg * GK3 + GK3 - 1
                            pk = (g4, qs, j - 1, kcl) if j > 0 else (g4, qs - 1, C.DT - 1, kcl)
                            gate2 = [("pe", mk.get(("w3kc",) + pk)),
                                     (f"dn{p_}", mk.get(("dnqp", g4, qs, j, p_),
                                                        mk.get(("dnqpl", p_)))),
                                     ("dve", mk[("w12don", g4)])]
                            for kc in range(kg * GK3, (kg + 1) * GK3):
                                em.op("sync", gate2 if kc == kg * GK3 else [],
                                      lambda kc=kc, j=j: nc.sync.dma_start(
                                          out=w3s[:, kc * 512:(kc + 1) * 512],
                                          in_=t["w3b"][j * 512:(j + 1) * 512,
                                                       kc * 128:(kc + 1) * 128],
                                          transpose=True), f"dn{p_}", 16)
                            mk[("dnqp", g4, qs, j, p_)] = em.c[f"dn{p_}"]
                            mk[("dnqpl", p_)] = em.c[f"dn{p_}"]
                            for kc in range(kg * GK3, (kg + 1) * GK3):
                                mk[("w3ld", g4, qs, j, kc)] = (f"dn{p_}", em.c[f"dn{p_}"])
                        GKo = min(8, C.DC)
                        for kg in range(C.DC // GKo):
                            p_ = kg % 2
                            kcl = kg * GKo + GKo - 1
                            pk = (g4, qs, j - 1, kcl) if j > 0 else (g4, qs - 1, C.DT - 1, kcl)
                            ogate = [("pe", mk.get(("owkc",) + pk)),
                                     (f"do{p_}", mk.get(("doqp", g4, qs, j, p_),
                                                        mk.get(("doqpl", p_))))]
                            for kc in range(kg * GKo, (kg + 1) * GKo):
                                em.op("sync", ogate if kc == kg * GKo else [],
                                      lambda kc=kc, j=j: nc.sync.dma_start(
                                          out=ows[:, kc * 512:(kc + 1) * 512],
                                          in_=t["owb"].rearrange("(kc p) n -> p kc n", p=128)
                                          [:, kc, j * 512:(j + 1) * 512]), f"do{p_}", 16)
                            mk[("doqp", g4, qs, j, p_)] = em.c[f"do{p_}"]
                            mk[("doqpl", p_)] = em.c[f"do{p_}"]
                            for kc in range(kg * GKo, (kg + 1) * GKo):
                                mk[("owld", g4, qs, j, kc)] = (f"do{p_}", em.c[f"do{p_}"])
                        for pi, wk in enumerate(("w3", "ow")):
                            nkc = C.MC if wk == "w3" else C.DC
                            for kc in range(nkc):
                                for qb in range(C.QB):
                                    bank = (j % 2) * 4 + qb
                                    if wk == "w3":
                                        deps = [mk[("w3ld", g4, qs, j, kc)]]
                                        if pi == 0 and kc == 0:
                                            pq, pj = (qs, j - 2) if j >= 2 else (qs - 1, C.DT + j - 2)
                                            deps.append(("dve", mk.get(("oev", g4, pq, pj, qb))))
                                        markit = (qb == C.QB - 1)
                                        v = em.op(
                                            "tensor", deps,
                                            lambda bank=bank, kc=kc, qb=qb, pi=pi, qs=qs:
                                            nc.tensor.matmul(
                                                ops_[bank][:, 0:512],
                                                x4[:, kc * C.GP4 + qs * C.QSB + qb * 128:
                                                   kc * C.GP4 + qs * C.QSB + qb * 128 + 128],
                                                w3s[:, kc * 512:(kc + 1) * 512],
                                                start=(pi == 0 and kc == 0), stop=False),
                                            "pe" if markit else None)
                                        if markit:
                                            mk[("w3kc", g4, qs, j, kc)] = v
                                    else:
                                        deps = [mk[("owld", g4, qs, j, kc)],
                                                ("dc0", mk[("cxs", g4, qs)])]
                                        lastmm = kc == nkc - 1
                                        markit = (qb == C.QB - 1) or lastmm
                                        v = em.op(
                                            "tensor", deps,
                                            lambda bank=bank, kc=kc, qb=qb, lastmm=lastmm:
                                            nc.tensor.matmul(
                                                ops_[bank][:, 0:512],
                                                cxs[:, kc * C.QSB + qb * 128:
                                                   kc * C.QSB + qb * 128 + 128],
                                                ows[:, kc * 512:(kc + 1) * 512],
                                                start=False, stop=lastmm),
                                            "pe" if markit else None)
                                        if qb == C.QB - 1:
                                            mk[("owkc", g4, qs, j, kc)] = v
                                        if lastmm:
                                            mk[("obank", g4, qs, j, qb)] = v
                        for qb in range(C.QB):
                            bank = (j % 2) * 4 + qb
                            r = ornd % 2
                            vt = em.op("vector",
                                       [("pe", mk[("obank", g4, qs, j, qb)]),
                                        (f"so{r}", mk.get((f"so{r}cnt", ornd - 2), 0))],
                                       lambda bank=bank, j=j, r=r: nc.vector.scalar_tensor_tensor(
                                           ot[r][:, :], ops_[bank][:, 0:512], 1.0,
                                           bsb[:, j * 512:(j + 1) * 512],
                                           ALU.mult, ALU.add), "dve")
                            mk[("oev", g4, qs, j, qb)] = vt
                            row0 = qs * C.QSB + qb * 128 + Q0
                            mk[(f"so{r}cnt", ornd)] = em.op(
                                "gpsimd", [("dve", vt), ("gx", 16)],
                                lambda row0=row0, j=j, r=r: nc.gpsimd.dma_start(
                                    out=t["out"][row0:row0 + 128, j * 512:(j + 1) * 512],
                                    in_=ot[r][:, :], accum_op=ALU.add), f"so{r}", 16)
                            ornd += 1
                mk[("p4bdone", g4)] = mk[("obank", g4, C.NQS - 1, C.DT - 1, C.QB - 1)]


def build_nc(C: Cfg, debug=False):
    nc = bass.Bass("TRN2", target_bir_lowering=False)
    D, S, H, F, M = C.D, C.S, C.H, C.F, C.M

    t = {}
    t["x"] = nc.dram_tensor("x", [S, D], F32, kind="ExternalInput").ap()
    t["qkv_w"] = nc.dram_tensor("qkv_w", [D, 3 * D], F32, kind="ExternalInput").ap()
    t["qkv_b"] = nc.dram_tensor("qkv_b", [3 * D], F32, kind="ExternalInput").ap()
    t["out_w"] = nc.dram_tensor("out_w", [D, D], F32, kind="ExternalInput").ap()
    t["out_b"] = nc.dram_tensor("out_b", [D], F32, kind="ExternalInput").ap()
    t["w12"] = nc.dram_tensor("w12", [F, D], F32, kind="ExternalInput").ap()
    t["b12"] = nc.dram_tensor("b12", [F], F32, kind="ExternalInput").ap()
    t["w3"] = nc.dram_tensor("w3", [D, M], F32, kind="ExternalInput").ap()
    t["b3"] = nc.dram_tensor("b3", [D], F32, kind="ExternalInput").ap()
    t["cosk"] = nc.dram_tensor("cosk", [64, S], F32, kind="ExternalInput").ap()
    t["sink"] = nc.dram_tensor("sink", [64, S], F32, kind="ExternalInput").ap()
    t["maskA"] = nc.dram_tensor("maskA", [C.CPT, 128, C.TT], BF16, kind="ExternalInput").ap()
    t["maskB"] = nc.dram_tensor("maskB", [C.CPT, 128, C.TT], BF16, kind="ExternalInput").ap()
    t["out"] = nc.dram_tensor("out", [C.OWN, D], F32, kind="ExternalOutput").ap()

    ik = "ExternalOutput" if debug else "Internal"
    t["qkvb"] = nc.dram_tensor("qkvb", [D, 3 * D], BF16, kind="Internal").ap()
    t["w12b"] = nc.dram_tensor("w12b", [F, D], BF16, kind="Internal").ap()
    t["w3b"] = nc.dram_tensor("w3b", [D, M], BF16, kind="Internal").ap()
    t["owb"] = nc.dram_tensor("owb", [D, D], BF16, kind="Internal").ap()
    HCG = min(4, H)
    t["kst"] = nc.dram_tensor("kst", [H // HCG, 2, HCG, 128, C.OWN], BF16, kind=ik).ap()
    t["vst"] = nc.dram_tensor("vst", [H // HCG, 2, HCG, 128, C.OWN], BF16, kind=ik).ap()
    t["kin"] = nc.dram_tensor("kin", [H, 128, C.OWN], BF16, kind="Internal").ap()
    t["vin"] = nc.dram_tensor("vin", [H, 128, C.OWN], BF16, kind="Internal").ap()
    t["qst"] = nc.dram_tensor("qst", [H, 128, C.OWN], BF16, kind=ik).ap()
    t["xnst"] = nc.dram_tensor("xnst", [C.DC, 128, C.OWN], BF16, kind=ik).ap()
    t["cxst"] = nc.dram_tensor("cxst", [H, 128, C.OWN], BF16, kind=ik).ap()

    ident16_h = nc.inline_tensor(np.eye(128, dtype=ml_dtypes.bfloat16), "c_ident16")
    ident32_h = nc.inline_tensor(np.eye(128, dtype=np.float32), "c_ident32")
    onesc_h = nc.inline_tensor(np.ones((128, 1), dtype=ml_dtypes.bfloat16), "c_onesc")
    onesr_h = nc.inline_tensor(np.ones((1, 128), dtype=np.float32), "c_onesr")
    epsc_h = nc.inline_tensor(np.full((128, 1), EPS, dtype=np.float32), "c_epsc")

    global _EM
    with contextlib.ExitStack() as top:
        em = Em(nc, SEMNAMES, top)
        _EM = em
        pr = {}
        pr["ident16"] = top.enter_context(nc.sbuf_tensor("ident16", [128, 128], BF16))
        pr["ident32"] = top.enter_context(nc.sbuf_tensor("ident32", [128, 128], F32))
        pr["onesc"] = top.enter_context(nc.sbuf_tensor("onesc", [128, 1], BF16))
        pr["onesr"] = top.enter_context(nc.sbuf_tensor("onesr", [1, 128], F32))
        pr["qbcol"] = top.enter_context(nc.sbuf_tensor("qbcol", [128, 3 * H], F32))
        pr["qbcolS"] = top.enter_context(nc.sbuf_tensor("qbcolS", [128, H], F32))
        pr["b12col"] = top.enter_context(nc.sbuf_tensor("b12col", [128, C.FC], F32))
        pr["epsc"] = top.enter_context(nc.sbuf_tensor("epsc", [128, 1], F32))

        def emit_all():
            mk = {}
            p0(nc, em, C, t, pr, mk, ident16_h, ident32_h, onesc_h, onesr_h, epsc_h)
            p1(nc, em, C, t, pr, mk)
            p2(nc, em, C, t, pr, mk)
            p4(nc, em, C, t, pr, mk)
            if em.on("gpsimd"):
                em.w("so0", em.c["so0"])
                em.w("so1", em.c["so1"])

        with nc.Block() as blk:
            for eng in ENGS:
                def body(E, eng=eng):
                    em.begin(eng, E)
                    emit_all()
                getattr(blk, eng)(body)
    return nc



def _rope_tables(S):
    half = 64
    inv = 1.0 / (ROPE_BASE ** (np.arange(half, dtype=np.float64) / half))
    ang = np.arange(S, dtype=np.float64)[:, None] * inv[None, :]
    return np.cos(ang).T.astype(np.float32), np.sin(ang).T.astype(np.float32)


def _masks(C, t):
    mA = np.zeros((C.CPT, 128, C.TT), dtype=np.float32)
    mB = np.zeros((C.CPT, 128, C.TT), dtype=np.float32)
    ii = np.arange(128)[:, None]
    jj = np.arange(C.TT)[None, :]
    dmask = np.stack([np.where(jj - ii >= 128 * d, 0.0, MASKVAL) for d in range(C.CPT)])
    if C.kvx:
        if t == 0:
            mA[:] = dmask
            mB[:] = MASKVAL
        else:
            mA[:] = 0.0
            mB[:] = dmask
    else:
        mA[:] = dmask
        mB[:] = 0.0 if t == 1 else MASKVAL
    return (mA.astype(ml_dtypes.bfloat16), mB.astype(ml_dtypes.bfloat16))


def stage_core(C, c, x_full, cos_t, sin_t):
    t = c % 2
    b = c // 2
    own = [2 * s + t for s in range(4)]
    par = [2 * s + 1 - t for s in range(4)]
    TT2 = C.S // 8
    rows = np.concatenate([np.arange(p * TT2, (p + 1) * TT2) for p in own + par])
    mA, mB = _masks(C, t)
    return {
        "x": np.ascontiguousarray(x_full[b][rows]),
        "cosk": np.ascontiguousarray(cos_t[:, rows]),
        "sink": np.ascontiguousarray(sin_t[:, rows]),
        "maskA": mA,
        "maskB": mB,
    }, rows


def kernel(L0_attn_hidden_states, L0_attn_qkv_w, L0_attn_qkv_b, L0_attn_out_w,
           L0_attn_out_b, L0_ffn_w12, L0_ffn_b12, L0_ffn_w3, L0_ffn_b3):
    C = CFG_FULL
    x = np.asarray(L0_attn_hidden_states, dtype=np.float32)
    cos_t, sin_t = _rope_tables(C.S)
    shared = {
        "qkv_w": np.asarray(L0_attn_qkv_w, dtype=np.float32),
        "qkv_b": np.asarray(L0_attn_qkv_b, dtype=np.float32),
        "out_w": np.asarray(L0_attn_out_w, dtype=np.float32),
        "out_b": np.asarray(L0_attn_out_b, dtype=np.float32),
        "w12": np.asarray(L0_ffn_w12, dtype=np.float32),
        "b12": np.asarray(L0_ffn_b12, dtype=np.float32),
        "w3": np.asarray(L0_ffn_w3, dtype=np.float32),
        "b3": np.asarray(L0_ffn_b3, dtype=np.float32),
    }
    nc = build_nc(C)
    in_maps = []
    rowmaps = []
    for c in range(8):
        m, rows = stage_core(C, c, x, cos_t, sin_t)
        m.update(shared)
        in_maps.append(m)
        rowmaps.append(rows)
    res = run_bass_kernel_spmd(nc, in_maps, core_ids=list(range(8)))
    out = np.empty((4, C.S, C.D), dtype=np.float32)
    for c in range(8):
        b = c // 2
        out[b][rowmaps[c][:C.OWN]] = res.results[c]["out"]
    return out
